# version 09
# baseline (speedup 1.0000x reference)
import numpy as np
from contextlib import ExitStack
import concourse.bass as bass
import concourse.mybir as mybir
from concourse.bass_utils import run_bass_kernel_spmd

F32 = mybir.dt.float32
BF16 = mybir.dt.bfloat16
I32 = mybir.dt.int32
AF = mybir.ActivationFunctionType
ALU = mybir.AluOpType
AX = mybir.AxisListType

T = 2048
E = 1024
NT = 16
NFC = 8
NKT = 16
NQC = 4
QW = 512
NE = 8
FH = 2048
NFH = 16

WOUT0 = 0
YP0 = 8192
VTA0 = 10240
VTB0 = 11280
PT0 = 12320
PT1 = 13344

C_ESC = 0
C_EBI = 1
C_EPS = 2
C_INV = 3
C_NH = 4
C_ONE = 5
C_NEG = 6
C_BIG = 7
C_CAP = 8
C_HG = 9
C_TR = 10
EXPC = 12.0
SPARSE = True
CAP = 768
NST = CAP // 128


class Prog:
    def __init__(self):
        self.streams = {n: [] for n in ("sync", "gpsimd", "tensor", "scalar", "vector")}
        self.c = {"ld": 0, "gs": 0, "gl1": 0, "gl2": 0, "glx": 0, "glxA": 0, "glxB": 0,
                  "gliA": 0, "gliB": 0, "fl": 0, "pc": 0, "st": 0, "pe": 0, "act": 0, "dve": 0}

    def em(self, stream, fn, waits=(), inc=None, amt=1):
        ws = tuple((s, v) for (s, v) in waits if v and v > 0)
        self.streams[stream].append((fn, ws, inc, amt))
        if inc is None:
            return None
        self.c[inc] += amt
        return self.c[inc]


def build_nc(do_ffn=True, mode="full"):
    nc = bass.Bass("TRN2", target_bir_lowering=False, detect_race_conditions=False)
    if not do_ffn:
        mode = "attn"
    d_dbg1 = d_dbg2 = None
    if mode in ("attn", "pair0"):
        d_dbga = nc.dram_tensor("dbga", [128, 16384], BF16, kind="ExternalOutput").ap()
    if mode == "pair0":
        d_dbgb = nc.dram_tensor("dbgb", [128, 16384], BF16, kind="ExternalOutput").ap()
        d_dbgr = nc.dram_tensor("dbgr", [128, 144], F32, kind="ExternalOutput").ap()
        d_dbgs = nc.dram_tensor("dbgs", [128, 64], F32, kind="ExternalOutput").ap()
    if mode == "ln":
        d_dbg1 = nc.dram_tensor("dbg1", [128, 16384], BF16, kind="ExternalOutput").ap()
        d_dbg2 = nc.dram_tensor("dbg2", [128, 144], F32, kind="ExternalOutput").ap()
        d_dbg3 = nc.dram_tensor("dbg3", [128, 112], F32, kind="ExternalOutput").ap()
    d_x = nc.dram_tensor("x", [T, E], F32, kind="ExternalInput").ap()
    d_wq = nc.dram_tensor("wq", [64, 64], F32, kind="ExternalInput").ap()
    d_wk = nc.dram_tensor("wk", [64, 64], F32, kind="ExternalInput").ap()
    d_wvt = nc.dram_tensor("wvt", [64, 64], F32, kind="ExternalInput").ap()
    d_woutt = nc.dram_tensor("woutt", [E, E], F32, kind="ExternalInput").ap()
    d_bout = nc.dram_tensor("bout", [1, E], F32, kind="ExternalInput").ap()
    d_wr = nc.dram_tensor("wr", [E, NE], F32, kind="ExternalInput").ap()
    d_br = nc.dram_tensor("br", [1, NE], F32, kind="ExternalInput").ap()
    d_w1 = nc.dram_tensor("w1", [NE, E, FH], F32, kind="ExternalInput").ap()
    d_b1t = nc.dram_tensor("b1t", [NE, 128, NFH], F32, kind="ExternalInput").ap()
    d_w2 = nc.dram_tensor("w2", [NE, FH, E], F32, kind="ExternalInput").ap()
    d_b2 = nc.dram_tensor("b2", [1, NE * E], F32, kind="ExternalInput").ap()
    d_cst = nc.dram_tensor("cst", [128, 16], F32, kind="ExternalInput").ap()
    d_ident = nc.dram_tensor("identb", [128, 128], BF16, kind="ExternalInput").ap()
    d_identf = nc.dram_tensor("identf", [128, 128], F32, kind="ExternalInput").ap()
    d_onesb = nc.dram_tensor("onesb", [1, 128], BF16, kind="ExternalInput").ap()
    d_onesf = nc.dram_tensor("onesf", [65, 64], F32, kind="ExternalInput").ap()
    d_zb = nc.dram_tensor("zb", [64, 128], BF16, kind="ExternalInput").ap()
    d_onescol = nc.dram_tensor("onescol", [128, 1], BF16, kind="ExternalInput").ap()
    d_zf = nc.dram_tensor("zf", [128, 1120], F32, kind="ExternalInput").ap()
    d_stri = nc.dram_tensor("stri", [128, 128], F32, kind="ExternalInput").ap()
    d_ebase = nc.dram_tensor("ebase", [128, 128], F32, kind="ExternalInput").ap()
    ik = "ExternalOutput" if mode == "rt" else "Internal"
    d_xn2 = nc.dram_tensor("xn2_d", [T, E], BF16, kind=ik).ap()
    d_gy = nc.dram_tensor("gy_d", [NE * CAP + 128, E], BF16, kind=ik).ap()
    d_gxall = nc.dram_tensor("gxall_d", [NE * CAP + 128, E], BF16, kind=ik).ap()
    if mode == "rt":
        d_dbgf = nc.dram_tensor("dbgf", [128, 128], I32, kind="ExternalOutput").ap()
        d_dbgl = nc.dram_tensor("dbgl", [128, 128], F32, kind="ExternalOutput").ap()
        d_dbgw = nc.dram_tensor("dbgw", [128, 1120], F32, kind="ExternalOutput").ap()
        d_dbgi = nc.dram_tensor("dbgi", [128, 256], F32, kind="ExternalOutput").ap()
        d_dbgx = nc.dram_tensor("dbgx", [128, 1120], F32, kind="ExternalOutput").ap()
    d_zbig = nc.dram_tensor("zbig", [128, E], BF16, kind="ExternalInput").ap()
    d_w1b = nc.dram_tensor("w1b_d", [NE, E, FH], BF16, kind="Internal").ap()
    d_w2b = nc.dram_tensor("w2b_d", [NE, FH, E], BF16, kind="Internal").ap()
    d_out = nc.dram_tensor("out", [T, E], F32, kind="ExternalOutput").ap()

    p = Prog()
    es = ExitStack()
    with es:
        ec = es.enter_context
        x_sb = ec(nc.sbuf_tensor("x_sb", [128, NT * 1024], F32))
        xnT = ec(nc.sbuf_tensor("xnT", [128, 16384], BF16))
        aT = ec(nc.sbuf_tensor("aT", [128, 16384], BF16))
        arB = ec(nc.sbuf_tensor("arB", [128, 16384], BF16))
        h_sb = ec(nc.sbuf_tensor("h_sb", [128, 8192], BF16))
        xn_tmp = ec(nc.sbuf_tensor("xn_tmp", [128, 1024], F32))
        st1 = ec(nc.sbuf_tensor("st1", [128, 352], F32))
        lgA = ec(nc.sbuf_tensor("lgA", [128, 128], F32))
        gate_sb = None if SPARSE else ec(nc.sbuf_tensor("gate_sb", [128, 128], F32))
        lgM = ec(nc.sbuf_tensor("lgM", [128, 128], F32))
        is1t = ec(nc.sbuf_tensor("is1t", [128, 128], F32))
        is2t = ec(nc.sbuf_tensor("is2t", [128, 128], F32))
        consts = ec(nc.sbuf_tensor("consts", [128, 16], F32))
        ones_bf = ec(nc.sbuf_tensor("ones_bf", [1, 128], BF16))
        ones_f = ec(nc.sbuf_tensor("ones_f", [65, 64], F32))
        ident = ec(nc.sbuf_tensor("ident", [128, 128], BF16))
        identf = ec(nc.sbuf_tensor("identf_s", [128, 128], F32))
        Gp = ec(nc.sbuf_tensor("Gp", [128, 128], BF16))
        WvTp = ec(nc.sbuf_tensor("WvTp", [128, 128], BF16))
        wqb = ec(nc.sbuf_tensor("wqb", [128, 64], BF16))
        wkb = ec(nc.sbuf_tensor("wkb", [128, 64], BF16))
        Wr_bf = ec(nc.sbuf_tensor("Wr_bf", [128, 64], BF16))
        br_bf = ec(nc.sbuf_tensor("br_bf", [1, 8], BF16))
        bo_bf = ec(nc.sbuf_tensor("bo_bf", [1, 1024], BF16))
        b2e = ec(nc.sbuf_tensor("b2e", [1, 1024], BF16))
        b2eB = ec(nc.sbuf_tensor("b2eB", [1, 1024], BF16))
        b1t_s = ec(nc.sbuf_tensor("b1t_s", [128, 128], F32))
        fscr = ec(nc.sbuf_tensor("fscr", [128, 4], F32))
        stri = ec(nc.sbuf_tensor("stri_s", [128, 128], F32))
        ebase = ec(nc.sbuf_tensor("ebase_s", [128, 128], F32))
        flati = ec(nc.sbuf_tensor("flati", [128, 128], I32))
        gxb = ec(nc.sbuf_tensor("gxb", [128, 4096], BF16))
        gyb = ec(nc.sbuf_tensor("gyb", [128, 2048], BF16))
        z128f = ec(nc.sbuf_tensor("z128f", [128, 128], F32))

        psA = ec(nc.psum_tensor("psA", [128, 1024], F32))
        psB = ec(nc.psum_tensor("psB", [128, 1024], F32))
        psC = ec(nc.psum_tensor("psC", [128, 512], F32))
        psD = ec(nc.psum_tensor("psD", [128, 512], F32))
        psE = ec(nc.psum_tensor("psE", [128, 512], F32))
        psF = ec(nc.psum_tensor("psF", [128, 512], F32))

        sems = {}
        for name in ("ld", "gs", "gl1", "gl2", "glx", "glxA", "glxB", "gliA", "gliB", "fl", "pc", "st", "pe", "act", "dve"):
            sems[name] = ec(nc.semaphore("s_" + name))

        stx = h_sb[:, 2048:4288].bitcast(F32)
        st2 = st1
        rr = stx[:, 0:1024]
        wstp = h_sb[:, 4288:6528].bitcast(F32)
        FD_B = h_sb[:, 6528:7552]

        def dve_fence(dest=None):
            d = lgA if dest is None else dest
            return lambda e: e.tensor_add(d[:, 0:128], x_sb[:, 0:128], x_sb[:, 128:256])

        def act_fence(ffn=False):
            if ffn:
                return lambda e: e.copy(lgM[:, 0:128], x_sb[:, 0:128])
            return lambda e: e.copy(FD_B[:, 512:1024], x_sb[:, 0:512])

        x_ld = []
        for t in range(NT):
            x_ld.append(p.em(
                "sync",
                (lambda t: lambda e: e.dma_start(out=x_sb[:, t * 1024:(t + 1) * 1024],
                                                 in_=d_x[t * 128:(t + 1) * 128, :]))(t),
                inc="ld", amt=16))
        qkv_ld = p.c["ld"]
        for ex in range(NE):
            p.em("sync", (lambda ex: lambda e: e.dma_start(out=b1t_s[:, ex * 16:(ex + 1) * 16],
                                                           in_=d_b1t[ex, :, :]))(ex),
                 inc="ld", amt=16)

        wout_ld = None
        for fc in range(NFC):
            wout_ld = p.em(
                "gpsimd",
                (lambda fc: lambda e: e.dma_start(out=arB[:, WOUT0 + fc * 1024: WOUT0 + (fc + 1) * 1024],
                                                  in_=d_woutt[fc * 128:(fc + 1) * 128, :]))(fc),
                inc="gs", amt=16)
        for fc in range(NFC):
            wr_ld = p.em(
                "gpsimd",
                (lambda fc: lambda e: e.dma_start(out=Wr_bf[:, fc * 8:(fc + 1) * 8],
                                                  in_=d_wr[fc * 128:(fc + 1) * 128, :]))(fc),
                inc="gs", amt=16)
        br_ld = p.em("gpsimd", lambda e: e.dma_start(out=br_bf[:, :], in_=d_br[:, :]), inc="gs", amt=16)
        bo_ld = p.em("gpsimd", lambda e: e.dma_start(out=bo_bf[:, :], in_=d_bout[:, :]), inc="gs", amt=16)
        b2_ld = 0
        gs_total = p.c["gs"]
        wv_gl = wout_ld = wr_ld = br_ld = bo_ld = b2_ld = gs_total
        p.em("gpsimd", lambda e: e.dma_start(out=wqb[0:64, :], in_=d_wq[:, :]), inc="gs", amt=16)
        p.em("gpsimd", lambda e: e.dma_start(out=wqb[64:128, :], in_=d_wq[:, :]), inc="gs", amt=16)
        p.em("gpsimd", lambda e: e.dma_start(out=wkb[0:64, :], in_=d_wk[:, :]), inc="gs", amt=16)
        p.em("gpsimd", lambda e: e.dma_start(out=wkb[64:128, :], in_=d_wk[:, :]), inc="gs", amt=16)
        p.em("gpsimd", lambda e: e.dma_start(out=WvTp[0:64, 0:64], in_=d_wvt[:, :]), inc="gs", amt=16)
        wv_gl = p.em("gpsimd", lambda e: e.dma_start(out=WvTp[64:128, 64:128], in_=d_wvt[:, :]),
                     inc="gs", amt=16)

        p.em("sync", lambda e: e.dma_start(out=consts[:, :], in_=d_cst[:, :]), inc="ld", amt=16)
        p.em("sync", lambda e: e.dma_start(out=ident[:, :], in_=d_ident[:, :]), inc="ld", amt=16)
        p.em("sync", lambda e: e.dma_start(out=identf[:, :], in_=d_identf[:, :]), inc="ld", amt=16)
        p.em("sync", lambda e: e.dma_start(out=ones_bf[:, :], in_=d_onesb[:, :]), inc="ld", amt=16)
        p.em("sync", lambda e: e.dma_start(out=ones_f[:, :], in_=d_onesf[:, :]), inc="ld", amt=16)
        p.em("sync", lambda e: e.dma_start(out=Gp[0:64, 64:128], in_=d_zb[:, 0:64]), inc="ld", amt=16)
        p.em("sync", lambda e: e.dma_start(out=Gp[64:128, 0:64], in_=d_zb[:, 0:64]), inc="ld", amt=16)
        p.em("sync", lambda e: e.dma_start(out=WvTp[0:64, 64:128], in_=d_zb[:, 0:64]), inc="ld", amt=16)
        p.em("sync", lambda e: e.dma_start(out=WvTp[64:128, 0:64], in_=d_zb[:, 0:64]), inc="ld", amt=16)
        for kt in range(NKT):
            p.em("sync", (lambda kt: lambda e: e.dma_start(
                out=arB[:, VTA0 + kt * 65 + 64: VTA0 + kt * 65 + 65], in_=d_onescol[:, :]))(kt),
                 inc="ld", amt=16)
            p.em("sync", (lambda kt: lambda e: e.dma_start(
                out=arB[:, VTB0 + kt * 65 + 64: VTB0 + kt * 65 + 65], in_=d_onescol[:, :]))(kt),
                 inc="ld", amt=16)
        p.em("sync", lambda e: e.dma_start(out=st1[:, :], in_=d_zf[:, 0:352]), inc="ld", amt=16)
        p.em("sync", lambda e: e.dma_start(out=wstp[:, :], in_=d_zf[:, :]), inc="ld", amt=16)
        p.em("sync", lambda e: e.dma_start(out=stri[:, :], in_=d_stri[:, :]), inc="ld", amt=16)
        p.em("sync", lambda e: e.dma_start(out=ebase[:, :], in_=d_ebase[:, :]), inc="ld", amt=16)
        p.em("sync", lambda e: e.dma_start(out=z128f[:, :], in_=d_zf[:, 0:128]), inc="ld", amt=16)
        const_ld = p.c["ld"]
        pc_total = 0
        if SPARSE and do_ffn:
            for ex in range(NE):
                for fc in range(NFC):
                    pc_total = p.em(
                        "gpsimd",
                        (lambda ex, fc: lambda e: e.dma_start(
                            out=d_w1b[ex, fc * 128:(fc + 1) * 128, :],
                            in_=d_w1[ex, fc * 128:(fc + 1) * 128, :]))(ex, fc),
                        waits=([("ld", const_ld)] if pc_total == 0 else ()),
                        inc="pc", amt=16)
                for fhc in range(NFH):
                    pc_total = p.em(
                        "gpsimd",
                        (lambda ex, fhc: lambda e: e.dma_start(
                            out=d_w2b[ex, fhc * 128:(fhc + 1) * 128, :],
                            in_=d_w2[ex, fhc * 128:(fhc + 1) * 128, :]))(ex, fhc),
                        inc="pc", amt=16)
        fl_total = 0
        if SPARSE:
            for i in range(NE * NST + 1):
                fl_total = p.em(
                    "sync",
                    (lambda i: lambda e: e.dma_start(out=d_gxall[i * 128:(i + 1) * 128, :],
                                                     in_=d_zbig[:, :]))(i),
                    inc="fl", amt=16)
            fl_total = p.em(
                "sync", lambda e: e.dma_start(out=d_gy[NE * CAP:NE * CAP + 128, :],
                                              in_=d_zbig[:, :]), inc="fl", amt=16)
        casts_done = 0

        p.em("tensor", lambda e: e.matmul(psC[0:64, 0:64], wkb[0:64, :], wqb[0:64, :],
                                          start=True, stop=True, tile_position=(0, 0)),
             waits=[("gs", wv_gl), ("ld", const_ld)])
        g_mm = p.em("tensor", lambda e: e.matmul(psC[64:128, 64:128], wkb[64:128, :], wqb[64:128, :],
                                                 start=True, stop=True, tile_position=(64, 64)), inc="pe")
        p.em("scalar", lambda e: e.copy(Gp[0:64, 0:64], psC[0:64, 0:64]), waits=[("pe", g_mm)])
        g_done = p.em("scalar", lambda e: e.copy(Gp[64:128, 64:128], psC[64:128, 64:128]), inc="act")
        psC_free_act = g_done
        psC_free_dve = 0
        psD_free_act = 0
        psD_free_dve = 0

        def emit_ln(stats, x_wait_sem, x_waits, dst_T, pe_extra_wait, dump_xn2=False):
            for t in range(NT):
                p.em("vector",
                     (lambda t: lambda e: e.tensor_mul(h_sb[:, 0:1024],
                                                       x_sb[:, t * 1024:(t + 1) * 1024],
                                                       x_sb[:, t * 1024:(t + 1) * 1024]))(t),
                     waits=[(x_wait_sem, x_waits[t]), ("ld", const_ld)])
                p.em("vector",
                     (lambda t: lambda e: e.reduce_sum(stats[:, t:t + 1], h_sb[:, 0:1024], AX.X))(t))
                p.em("vector",
                     (lambda t: lambda e: e.reduce_sum(stats[:, 176 + t:177 + t],
                                                       x_sb[:, t * 1024:(t + 1) * 1024], AX.X))(t))
            p.em("vector", lambda e: e.tensor_scalar(stx[:, 0:160], stats[:, 176:336],
                                                     consts[:, C_INV:C_INV + 1], None, ALU.mult),
                 waits=[("ld", const_ld)])
            p.em("vector", lambda e: e.tensor_scalar(stx[:, 160:320], stats[:, 0:160],
                                                     consts[:, C_INV:C_INV + 1], None, ALU.mult))
            p.em("vector", lambda e: e.tensor_mul(stx[:, 320:480], stx[:, 0:160], stx[:, 0:160]))
            p.em("vector", lambda e: e.tensor_sub(stx[:, 480:640], stx[:, 160:320], stx[:, 320:480]))
            var_done = p.em("vector", dve_fence(), inc="dve")
            p.em("scalar", lambda e: e.activation(stx[:, 640:800], stx[:, 480:640], AF.Ln,
                                                  bias=consts[:, C_EPS:C_EPS + 1]),
                 waits=[("dve", var_done), ("ld", const_ld)])
            p.em("scalar", lambda e: e.activation(stx[:, 800:960], stx[:, 640:800], AF.Exp,
                                                  scale=consts[:, C_NH:C_NH + 1]))
            rstd_f = p.em("scalar", act_fence(), inc="act")
            p.em("vector", lambda e: e.scalar_tensor_tensor(stx[:, 960:1120], stx[:, 0:160],
                                                            consts[:, C_NEG:C_NEG + 1],
                                                            stx[:, 800:960], ALU.mult, ALU.mult),
                 waits=[("act", rstd_f)])
            nb_done = p.em("vector", dve_fence(), inc="dve")
            evac = [[0] * NFC for _ in range(NT)]
            norm = [0] * NT
            tp_prev = [0, 0]
            last_tp = 0
            for t in range(NT):
                w = [("dve", nb_done)]
                if t > 0:
                    w.append(("pe", last_tp))
                    if dump_xn2:
                        w.append(("glx", 16 * t))
                p.em("scalar",
                     (lambda t: lambda e: e.activation(xn_tmp[:, :], x_sb[:, t * 1024:(t + 1) * 1024],
                                                       AF.Identity, bias=stx[:, 960 + t:961 + t],
                                                       scale=stx[:, 800 + t:801 + t]))(t),
                     waits=w)
                norm[t] = p.em("scalar", act_fence(), inc="act")
                if dump_xn2:
                    p.em("gpsimd",
                         (lambda t: lambda e: e.dma_start(out=d_xn2[t * 128:(t + 1) * 128, :],
                                                          in_=xn_tmp[:, :]))(t),
                         waits=[("act", norm[t])], inc="glx", amt=16)
                tp_cnt = [0] * (NFC + 1)
                for fc in range(NFC + 1):
                    par = fc % 2
                    ps = psE if par == 0 else psF
                    w = [("act", norm[t]), (("act", "dve")[par], tp_prev[par]), ("ld", const_ld)]
                    if t == 0 and fc == 0 and pe_extra_wait is not None:
                        w.append(pe_extra_wait)
                    src_col = (fc % NFC) * 128
                    tp_cnt[fc] = last_tp = p.em(
                        "tensor",
                        (lambda ps, src_col: lambda e: e.transpose(
                            ps[:, 0:128], xn_tmp[:, src_col:src_col + 128], identf[:, :]))(ps, src_col),
                        waits=w, inc="pe")
                    if fc >= 1:
                        pfc = fc - 1
                        pps = psE if pfc % 2 == 0 else psF
                        eng, snm = ("scalar", "act") if pfc % 2 == 0 else ("vector", "dve")
                        tp_prev[pfc % 2] = evac[t][pfc] = p.em(
                            eng,
                            (lambda pps, t, pfc: lambda e: (e.tensor_copy if hasattr(e, 'tensor_copy') else e.copy)(
                                dst_T[:, pfc * 2048 + t * 128: pfc * 2048 + (t + 1) * 128],
                                pps[:, 0:128]))(pps, t, pfc),
                            waits=[("pe", tp_cnt[pfc + 1])], inc=snm)
            fin = p.em("vector", dve_fence(), inc="dve")
            evac[NT - 1][NFC - 1] = fin
            return evac, norm, tp_prev

        ln1_evac, _, _ = emit_ln(st1, "ld", [const_ld] * NT, xnT, None)
        xnT_done = ln1_evac[NT - 1][NFC - 1]

        if mode == "ln":
            for fc in range(NFC):
                p.em("sync",
                     (lambda fc: lambda e: e.dma_start(out=d_dbg1[:, fc * 2048:(fc + 1) * 2048],
                                                       in_=xnT[:, fc * 2048:(fc + 1) * 2048]))(fc),
                     waits=[("dve", xnT_done)], inc="st", amt=16)
            p.em("sync", lambda e: e.dma_start(out=d_dbg2[:, :], in_=st1[:, :]),
                 waits=[("dve", xnT_done)], inc="st", amt=16)
            p.em("sync", lambda e: e.dma_start(out=d_dbg3[:, :], in_=wst[:, :]),
                 waits=[("dve", xnT_done)], inc="st", amt=16)
            p.em("sync", lambda e: e.wait_ge(sems["st"], p.c["st"]))
            emit_block(nc, p, sems)
            return nc

        bankA_free_act = 0
        bankB_free_act = 0
        yp_free = 0
        vt_free = 0
        pt_free = [0, 0]

        sc_last = 0
        av_last = 0
        rr_lastread = [0, 0]
        oa_mul_free = [0, 0]
        tmp_mul_free = [0, 0]
        pending = None
        mul_counts = []

        def emit_epilogue(ctx):
            nonlocal rr_lastread, mul_counts
            hp, qc, par, oA, oB, av15 = ctx
            colA = hp * 2048 + qc * 512
            r0 = 64 * par
            evA = p.em("scalar",
                       (lambda oA, r0: lambda e: e.copy(xn_tmp[r0:r0 + 64, 0:512], oA[0:64, 0:512]))(oA, r0),
                       waits=[("pe", av15), ("dve", tmp_mul_free[par])], inc="act")
            evB = p.em("scalar",
                       (lambda oB, r0: lambda e: e.copy(xn_tmp[r0:r0 + 64, 512:1024], oB[0:64, 0:512]))(oB, r0),
                       inc="act")
            rcA = p.em("vector",
                       (lambda oA: lambda e: e.reciprocal(rr[64:65, 0:512], oA[64:65, 0:512]))(oA),
                       waits=[("pe", av15), ("pe", rr_lastread[0])], inc="dve")
            rcB = p.em("vector",
                       (lambda oB: lambda e: e.reciprocal(rr[64:65, 512:1024], oB[64:65, 0:512]))(oB),
                       waits=[("pe", rr_lastread[1])], inc="dve")
            rbA = p.em("tensor",
                       (lambda oA: lambda e: e.matmul(oA[0:64, 0:512], ones_f[64:65, 0:64],
                                                      rr[64:65, 0:512], start=True, stop=True,
                                                      skip_group_check=True))(oA),
                       waits=[("act", evA), ("dve", rcA), ("ld", const_ld)], inc="pe")
            rbB = p.em("tensor",
                       (lambda oB: lambda e: e.matmul(oB[0:64, 0:512], ones_f[64:65, 0:64],
                                                      rr[64:65, 512:1024], start=True, stop=True,
                                                      skip_group_check=True))(oB),
                       waits=[("dve", rcB)], inc="pe")
            rr_lastread = [rbA, rbB]
            muA = p.em("vector",
                       (lambda colA, oA, r0: lambda e: e.tensor_mul(
                           aT[0:64, colA:colA + 512], xn_tmp[r0:r0 + 64, 0:512],
                           oA[0:64, 0:512]))(colA, oA, r0),
                       waits=[("pe", rbA), ("act", evB)], inc="dve")
            muB = p.em("vector",
                       (lambda colA, oB, r0: lambda e: e.tensor_mul(
                           aT[64:128, colA:colA + 512], xn_tmp[r0:r0 + 64, 512:1024],
                           oB[0:64, 0:512]))(colA, oB, r0),
                       waits=[("pe", rbB)], inc="dve")
            mul_counts.append((muA, muB))
            oa_mul_free[par] = muB
            tmp_mul_free[par] = muB

        gqc = 0
        for hp in range(1 if mode == 'pair0' else NFC):
            ym = []
            for i in range(4):
                ps = (psA, psB)[i // 2]
                col = (i % 2) * 512
                w = [("act", g_done), ("dve", xnT_done),
                     ("act", bankA_free_act if i // 2 == 0 else bankB_free_act)]
                ym.append(p.em(
                    "tensor",
                    (lambda ps, col, hp, i: lambda e: e.matmul(
                        ps[:, col:col + 512], Gp[:, :],
                        xnT[:, hp * 2048 + i * 512: hp * 2048 + (i + 1) * 512],
                        start=True, stop=True))(ps, col, hp, i),
                    waits=w, inc="pe"))
            ye = []
            for i in range(4):
                ps = (psA, psB)[i // 2]
                col = (i % 2) * 512
                ye.append(p.em(
                    "scalar",
                    (lambda ps, col, i: lambda e: e.copy(
                        arB[:, YP0 + i * 512: YP0 + (i + 1) * 512], ps[:, col:col + 512]))(ps, col, i),
                    waits=[("pe", ym[i]), ("pe", yp_free)], inc="act"))
            bankA_free_act = ye[1]
            bankB_free_act = ye[3]

            ve_prev = [bankA_free_act, bankB_free_act]
            ve_last = 0
            for tt in range(NT):
                ps = (psA, psB)[tt % 2]
                w = [("gs", wv_gl), ("ld", const_ld), ("dve", xnT_done),
                     ("act", ve_prev[tt % 2]), ("pe", vt_free)]
                vm = p.em(
                    "tensor",
                    (lambda ps, hp, tt: lambda e: e.matmul(
                        ps[:, 0:128], xnT[:, hp * 2048 + tt * 128: hp * 2048 + (tt + 1) * 128],
                        WvTp[:, :], start=True, stop=True))(ps, hp, tt),
                    waits=w, inc="pe")
                p.em("scalar",
                     (lambda ps, tt: lambda e: e.copy(arB[:, VTA0 + tt * 65: VTA0 + tt * 65 + 64],
                                                      ps[:, 0:64]))(ps, tt),
                     waits=[("pe", vm)], inc="act")
                ve_last = p.em(
                    "scalar",
                    (lambda ps, tt: lambda e: e.copy(arB[:, VTB0 + tt * 65: VTB0 + tt * 65 + 64],
                                                     ps[:, 64:128]))(ps, tt),
                    inc="act")
                ve_prev[tt % 2] = ve_last
            bankA_free_act = ve_prev[0]
            bankB_free_act = ve_prev[1]

            for qc in range(NQC):
                par = gqc % 2
                oA, oB = ((psC, psD), (psE, psF))[par]
                sc = [0] * NKT
                expd = [0] * NKT
                avd = [0] * NKT

                def emit_scores(kt):
                    nonlocal bankA_free_act, bankB_free_act, sc_last
                    ps = (psA, psB)[kt % 2]
                    bfree = bankA_free_act if kt % 2 == 0 else bankB_free_act
                    w = [("act", bfree), ("act", ve_last), ("act", ye[3])]
                    p.em("tensor",
                         (lambda ps, hp, kt, qc: lambda e: e.matmul(
                             ps[:, 0:512], arB[0:64, YP0 + kt * 128: YP0 + (kt + 1) * 128],
                             xnT[0:64, hp * 2048 + qc * 512: hp * 2048 + (qc + 1) * 512],
                             start=True, stop=True, tile_position=(0, 0)))(ps, hp, kt, qc),
                         waits=w)
                    sc[kt] = sc_last = p.em(
                        "tensor",
                        (lambda ps, hp, kt, qc: lambda e: e.matmul(
                            ps[:, 512:1024], arB[64:128, YP0 + kt * 128: YP0 + (kt + 1) * 128],
                            xnT[64:128, hp * 2048 + qc * 512: hp * 2048 + (qc + 1) * 512],
                            start=True, stop=True, tile_position=(64, 0)))(ps, hp, kt, qc),
                        inc="pe")

                def emit_exp(kt):
                    nonlocal bankA_free_act, bankB_free_act
                    ps = (psA, psB)[kt % 2]
                    pt_col = PT0 if kt % 2 == 0 else PT1
                    expd[kt] = p.em(
                        "scalar",
                        (lambda ps, pt_col: lambda e: e.activation(
                            arB[:, pt_col:pt_col + 1024], ps[:, 0:1024], AF.Exp,
                            bias=consts[:, C_EBI:C_EBI + 1], scale=consts[:, C_ESC:C_ESC + 1]))(ps, pt_col),
                        waits=[("pe", sc[kt]), ("pe", pt_free[kt % 2])], inc="act")
                    if kt % 2 == 0:
                        bankA_free_act = expd[kt]
                    else:
                        bankB_free_act = expd[kt]

                def emit_av(kt):
                    nonlocal av_last
                    pt_col = PT0 if kt % 2 == 0 else PT1
                    w = [("act", expd[kt])]
                    if kt == 0:
                        w += [("dve", oa_mul_free[par])]
                    p.em("tensor",
                         (lambda kt, pt_col, oA: lambda e: e.matmul(
                             oA[0:65, 0:512], arB[:, VTA0 + kt * 65: VTA0 + (kt + 1) * 65],
                             arB[:, pt_col:pt_col + 512],
                             start=(kt == 0), stop=(kt == NKT - 1), skip_group_check=True))(kt, pt_col, oA),
                         waits=w)
                    avd[kt] = av_last = p.em(
                        "tensor",
                        (lambda kt, pt_col, oB: lambda e: e.matmul(
                            oB[0:65, 0:512], arB[:, VTB0 + kt * 65: VTB0 + (kt + 1) * 65],
                            arB[:, pt_col + 512:pt_col + 1024],
                            start=(kt == 0), stop=(kt == NKT - 1), skip_group_check=True))(kt, pt_col, oB),
                        inc="pe")
                    pt_free[kt % 2] = av_last

                for kt in range(NKT):
                    emit_scores(kt)
                    emit_exp(kt)
                    if kt > 0:
                        emit_av(kt - 1)
                emit_av(NKT - 1)

                if pending is not None:
                    emit_epilogue(pending)
                pending = (hp, qc, par, oA, oB, avd[NKT - 1])
                gqc += 1
            yp_free = sc_last
            vt_free = av_last

        emit_epilogue(pending)
        attn_mul_done = p.em("vector", dve_fence(), inc="dve")
        xnT_lastread_pe = sc_last

        if mode == "pair0":
            W = [("dve", attn_mul_done)]
            for fc in range(NFC):
                p.em("sync",
                     (lambda fc: lambda e: e.dma_start(out=d_dbga[:, fc * 2048:(fc + 1) * 2048],
                                                       in_=xnT[:, fc * 2048:(fc + 1) * 2048]))(fc),
                     waits=W, inc="st", amt=16)
            for src_ap, c0, w_ in ((arB[:, YP0:YP0 + 2048], 0, 2048),
                                   (arB[:, VTA0:VTA0 + 1040], 2048, 1040),
                                   (arB[:, VTB0:VTB0 + 1040], 3088, 1040),
                                   (arB[:, PT0:PT0 + 1024], 4128, 1024),
                                   (arB[:, PT1:PT1 + 1024], 5152, 1024),
                                   (aT[:, 0:2048], 6176, 2048),
                                   (Gp[:, :], 8224, 128),
                                   (WvTp[:, :], 8352, 128),
                                   (ident[:, :], 8480, 128)):
                p.em("sync",
                     (lambda src_ap, c0, w_: lambda e: e.dma_start(out=d_dbgb[:, c0:c0 + w_], in_=src_ap))(src_ap, c0, w_),
                     waits=W, inc="st", amt=16)
            p.em("sync", lambda e: e.dma_start(out=d_dbgr[:, 0:32], in_=st1[:, 0:32]),
                 waits=W, inc="st", amt=16)
            p.em("sync", lambda e: e.dma_start(out=d_dbgs[:, 0:8], in_=consts[:, :]),
                 waits=W, inc="st", amt=16)
            p.em("sync", lambda e: e.dma_start(out=d_dbgs[:, 8:40], in_=stx[:, 160:192]),
                 waits=W, inc="st", amt=16)
            p.em("sync", lambda e: e.wait_ge(sems["st"], p.c["st"]))
            emit_block(nc, p, sems)
            return nc

        op_x = [0] * NT
        op_bank_free = [attn_mul_done, attn_mul_done]
        opA_free_act = [bankA_free_act, bankB_free_act]
        dp_last = 0
        for qt in range(NT):
            ps = (psA, psB)[qt % 2]
            for fo in range(2):
                w = [("gs", wout_ld), ("gs", bo_ld), ("act", opA_free_act[qt % 2]),
                     ("dve", op_bank_free[qt % 2]), ("dve", attn_mul_done)]
                p.em("tensor",
                     (lambda ps, fo: lambda e: e.matmul(ps[:, fo * 512:(fo + 1) * 512],
                                                        ones_bf[0:1, 0:128], bo_bf[0:1, fo * 512:(fo + 1) * 512],
                                                        start=True, stop=False))(ps, fo),
                     waits=w)
                for fc in range(NFC):
                    dp_last = p.em(
                        "tensor",
                        (lambda ps, fo, fc, qt: lambda e: e.matmul(
                            ps[:, fo * 512:(fo + 1) * 512],
                            aT[:, fc * 2048 + qt * 128: fc * 2048 + (qt + 1) * 128],
                            arB[:, WOUT0 + fc * 1024 + fo * 512: WOUT0 + fc * 1024 + fo * 512 + 512],
                            start=False, stop=(fc == NFC - 1)))(ps, fo, fc, qt),
                        inc="pe")
            opA_free_act[qt % 2] = 0
            op_bank_free[qt % 2] = op_x[qt] = p.em(
                "vector",
                (lambda ps, qt: lambda e: e.scalar_tensor_tensor(
                    x_sb[:, qt * 1024:(qt + 1) * 1024], ps[:, 0:1024], consts[:, C_ONE:C_ONE + 1],
                    x_sb[:, qt * 1024:(qt + 1) * 1024], ALU.mult, ALU.add))(ps, qt),
                waits=[("pe", dp_last)], inc="dve")
        outproj_pe_done = dp_last

        if not do_ffn:
            for fc in range(NFC):
                p.em("sync",
                     (lambda fc: lambda e: e.dma_start(out=d_dbga[:, fc * 2048:(fc + 1) * 2048],
                                                       in_=aT[:, fc * 2048:(fc + 1) * 2048]))(fc),
                     waits=[("dve", attn_mul_done)], inc="st", amt=16)
            for t in range(NT):
                p.em("sync",
                     (lambda t: lambda e: e.dma_start(out=d_out[t * 128:(t + 1) * 128, :],
                                                      in_=x_sb[:, t * 1024:(t + 1) * 1024]))(t),
                     waits=[("dve", op_x[t])], inc="st", amt=16)
            p.em("sync", lambda e: e.wait_ge(sems["st"], p.c["st"]))
            emit_block(nc, p, sems)
        else:
            op_f = p.em("vector", dve_fence(), inc="dve")
            ln2_waits = [op_x[t + 1] for t in range(NT - 1)] + [op_f]
            ln2_evac, _, ln2_tp = emit_ln(st2, "dve", ln2_waits, aT, ("dve", attn_mul_done),
                                          dump_xn2=SPARSE)
            xn2_done = [ln2_evac[t][NFC - 1] for t in range(NT)]

            def bc3(ap16):
                return bass.AP(ap16.tensor, ap16.offset, list(ap16.ap) + [[0, 8]])

            def t3(ap128):
                stp, cnt = ap128.ap[0]
                return bass.AP(ap128.tensor, ap128.offset, [[stp, cnt], [8, 16], [1, 8]])

            rt_ps_free = [0, 0]
            lg_mm = [0] * NT
            for t in range(NT):
                ps = (psC, psD)[t % 2]
                w = [("gs", br_ld), ("gs", wr_ld), ("dve", xn2_done[t]),
                     ("dve", rt_ps_free[t % 2])]
                p.em("tensor",
                     (lambda ps: lambda e: e.matmul(ps[:, 0:8], ones_bf[0:1, 0:128], br_bf[0:1, 0:8],
                                                    start=True, stop=False))(ps),
                     waits=w)
                for fc in range(NFC):
                    lg_mm[t] = p.em(
                        "tensor",
                        (lambda ps, fc, t: lambda e: e.matmul(
                            ps[:, 0:8], aT[:, fc * 2048 + t * 128: fc * 2048 + (t + 1) * 128],
                            Wr_bf[:, fc * 8:(fc + 1) * 8], start=False, stop=(fc == NFC - 1)))(ps, fc, t),
                        inc="pe")
                c0 = t * 8
                rt_ps_free[t % 2] = p.em(
                    "vector",
                    (lambda ps, c0: lambda e: e.tensor_copy(lgA[:, c0:c0 + 8], ps[:, 0:8]))(ps, c0),
                    waits=[("pe", lg_mm[t])], inc="dve")
            lg_f = p.em("vector", dve_fence(lgM), inc="dve")
            lg_dump = 0
            if mode == "rt":
                lg_dump = p.em("sync", lambda e: e.dma_start(out=d_dbgl[:, :], in_=lgA[:, :]),
                               waits=[("dve", lg_f)], inc="st", amt=16)
            p.em("vector", lambda e: e.reduce_max(wstp[:, 0:16], t3(lgA[:, :]), AX.X))
            p.em("vector", lambda e: e.tensor_tensor(t3(is1t[:, :]), t3(lgA[:, :]),
                                                     bc3(wstp[:, 0:16]), ALU.is_ge))
            p.em("vector", lambda e: e.scalar_tensor_tensor(lgM[:, :], is1t[:, :],
                                                            consts[:, C_BIG:C_BIG + 1], lgA[:, :],
                                                            ALU.mult, ALU.add))
            p.em("vector", lambda e: e.reduce_max(wstp[:, 160:176], t3(lgM[:, :]), AX.X))
            p.em("vector", lambda e: e.tensor_tensor(t3(is2t[:, :]), t3(lgM[:, :]),
                                                     bc3(wstp[:, 160:176]), ALU.is_ge))
            p.em("vector", lambda e: e.tensor_sub(wstp[:, 320:480], wstp[:, 0:160], wstp[:, 160:320]))
            d_done = p.em("vector", dve_fence(), waits=([("st", lg_dump)] if lg_dump else ()),
                          inc="dve")
            p.em("scalar", lambda e: e.activation(wstp[:, 480:640], wstp[:, 320:480], AF.Exp,
                                                  scale=consts[:, C_NEG:C_NEG + 1]),
                 waits=[("dve", d_done)])
            e_done = p.em("scalar", act_fence(), inc="act")
            p.em("vector", lambda e: e.tensor_scalar(wstp[:, 640:800], wstp[:, 480:640],
                                                     consts[:, C_ONE:C_ONE + 1], None, ALU.add),
                 waits=[("act", e_done)])
            p.em("vector", lambda e: e.reciprocal(wstp[:, 800:960], wstp[:, 640:800]))
            w_done = p.em("vector", lambda e: e.tensor_scalar(wstp[:, 960:1120], wstp[:, 800:960],
                                                              consts[:, C_ONE:C_ONE + 1],
                                                              consts[:, C_NEG:C_NEG + 1],
                                                              ALU.subtract, ALU.mult), inc="dve")
            if not SPARSE:
                p.em("vector", lambda e: e.tensor_tensor(t3(lgM[:, :]), t3(is1t[:, :]),
                                                         bc3(wstp[:, 800:816]), ALU.mult))
                p.em("vector", lambda e: e.tensor_tensor(t3(gate_sb[:, :]), t3(is2t[:, :]),
                                                         bc3(wstp[:, 960:976]), ALU.mult))
                gate_final = p.em("vector", lambda e: e.tensor_tensor(gate_sb[:, :], gate_sb[:, :],
                                                                      lgM[:, :], ALU.add), inc="dve")
                gb_x = [gate_final] * NT
            else:
                rFD = FD_B.bitcast(F32)
                r_ping = stx[:, 0:192]
                r_pong = stx[:, 192:384]
                p.em("vector", lambda e: e.tensor_sub(stx[:, 0:64], ebase[:, 0:64], ebase[:, 0:64]),
                     waits=[("dve", w_done), ("ld", const_ld)])
                p.em("vector", lambda e: e.tensor_sub(stx[:, 192:256], ebase[:, 0:64], ebase[:, 0:64]))
                p.em("vector", lambda e: e.tensor_add(stx[:, 64:192], is1t[:, :], is2t[:, :]))
                p.em("vector", lambda e: e.tensor_add(r_pong[:, 64:192], r_ping[:, 64:192],
                                                      r_ping[:, 56:184]))
                p.em("vector", lambda e: e.tensor_add(r_ping[:, 64:192], r_pong[:, 64:192],
                                                      r_pong[:, 48:176]))
                p.em("vector", lambda e: e.tensor_add(r_pong[:, 64:192], r_ping[:, 64:192],
                                                      r_ping[:, 32:160]))
                p.em("vector", lambda e: e.tensor_add(r_ping[:, 64:192], r_pong[:, 64:192],
                                                      r_pong[:, 0:128]))
                p.em("vector", lambda e: e.tensor_sub(stx[:, 384:512], r_ping[:, 64:192], is1t[:, :]))
                cum_d = p.em("vector", lambda e: e.tensor_sub(stx[:, 512:640], stx[:, 384:512],
                                                              is2t[:, :]), inc="dve")
                crossm = p.em("tensor", lambda e: e.matmul(psE[:, 0:8], stri[:, :],
                                                           r_ping[:, 184:192], start=True, stop=True),
                              waits=[("dve", cum_d), ("ld", const_ld),
                                     ("act", ln2_tp[0]), ("dve", ln2_tp[1])], inc="pe")
                def bc8(ap8):
                    return bass.AP(ap8.tensor, ap8.offset, [list(ap8.ap[0]), [0, 16], [1, 8]])
                p.em("vector", lambda e: e.tensor_tensor(t3(stx[:, 640:768]), t3(stx[:, 512:640]),
                                                         bc8(psE[:, 0:8]), ALU.add),
                     waits=[("pe", crossm)])
                p.em("vector", lambda e: e.tensor_add(stx[:, 768:896], stx[:, 640:768], ebase[:, :]))
                p.em("vector", lambda e: e.tensor_scalar(stx[:, 896:1024], stx[:, 640:768],
                                                         consts[:, C_CAP:C_CAP + 1], None, ALU.is_ge))
                p.em("vector", lambda e: e.scalar_tensor_tensor(rFD[:, 0:128], stx[:, 896:1024],
                                                                consts[:, C_HG:C_HG + 1], stx[:, 768:896],
                                                                ALU.mult, ALU.add))
                p.em("vector", lambda e: e.tensor_scalar(rFD[:, 0:128], rFD[:, 0:128],
                                                         consts[:, C_TR:C_TR + 1], None, ALU.min))
                p.em("vector", lambda e: e.tensor_mul(rFD[:, 128:256], is1t[:, :], rFD[:, 0:128]))
                p.em("vector", lambda e: e.tensor_mul(rFD[:, 256:384], is2t[:, :], rFD[:, 0:128]))
                p.em("vector", lambda e: e.tensor_sub(rFD[:, 384:512], ebase[:, :], ebase[:, :]))
                p.em("vector", lambda e: e.reduce_sum(rFD[:, 384:400], t3(rFD[:, 128:256]), AX.X))
                p.em("vector", lambda e: e.reduce_sum(rFD[:, 400:416], t3(rFD[:, 256:384]), AX.X))
                flati_d = p.em("vector", lambda e: e.tensor_tensor(flati[:, :], rFD[:, 384:512],
                                                                   rFD[:, 384:512], ALU.max), inc="dve")
                flati_f = p.em("vector", dve_fence(), inc="dve")
                if mode == "rt":
                    p.em("sync", lambda e: e.dma_start(out=d_dbgf[:, :], in_=flati[:, :]),
                         waits=[("dve", flati_f)], inc="st", amt=16)
                    p.em("sync", lambda e: e.dma_start(out=d_dbgw[:, :], in_=wstp[:, :]),
                         inc="st", amt=16)
                    p.em("sync", lambda e: e.dma_start(out=d_dbgi[:, 0:128], in_=is1t[:, :]),
                         inc="st", amt=16)
                    p.em("sync", lambda e: e.dma_start(out=d_dbgi[:, 128:256], in_=is2t[:, :]),
                         inc="st", amt=16)
                    p.em("sync", lambda e: e.dma_start(out=d_dbgx[:, :], in_=stx[:, :]),
                         inc="st", amt=16)
                scatA = scatB = 0
                scat_prev = [0, 0, 0, 0]
                for t in range(NT):
                    par = t % 2
                    sl4 = t % 4
                    gsem = "gliA" if par == 0 else "gliB"
                    w = [("glx", 16 * NT), ("fl", fl_total), ("dve", flati_f)]
                    if t >= 4:
                        w.append((gsem, scat_prev[sl4]))
                    ld_c = p.em(
                        "gpsimd",
                        (lambda t, sl4: lambda e: e.dma_start(
                            out=gxb[:, sl4 * 1024:(sl4 + 1) * 1024],
                            in_=d_xn2[t * 128:(t + 1) * 128, :]))(t, sl4),
                        waits=w, inc=gsem, amt=16)
                    p.em("gpsimd",
                         (lambda t, sl4: lambda e: e.indirect_dma_start(
                             out=d_gxall[:, :],
                             out_offset=bass.IndirectOffsetOnAxis(flati[:, t:t + 1], 0),
                             in_=gxb[:, sl4 * 1024:(sl4 + 1) * 1024], in_offset=None))(t, sl4),
                         waits=[(gsem, ld_c)], inc=gsem, amt=16)
                    s2 = p.em(
                        "gpsimd",
                        (lambda t, sl4: lambda e: e.indirect_dma_start(
                            out=d_gxall[:, :],
                            out_offset=bass.IndirectOffsetOnAxis(flati[:, 16 + t:17 + t], 0),
                            in_=gxb[:, sl4 * 1024:(sl4 + 1) * 1024], in_offset=None))(t, sl4),
                        inc=gsem, amt=16)
                    scat_prev[sl4] = s2
                    if par == 0:
                        scatA = s2
                    else:
                        scatB = s2

            h_free_pe = 0
            w1_ld = [0] * NE
            w2_ld = [0] * NE
            h_last_pe = [0] * NE
            y_last_pe = [0] * NE
            relu_last = [0] * (NE + 2)
            ybank_free = [rt_ps_free[0], rt_ps_free[1]]
            x_final = [[0, 0] for _ in range(NT)]
            hb_aps = [(psA, 0), (psA, 512), (psB, 0), (psB, 512)]
            hbank_free = [0, 0, 0, 0]
            hbank_free_a = [opA_free_act[0], opA_free_act[1], opA_free_act[0], opA_free_act[1]]
            hbank_free_d = [op_bank_free[0], op_bank_free[1], op_bank_free[0], op_bank_free[1]]
            gx_free = [0, 0, 0, 0]
            af_free = [0, 0]
            tp_bank_free = [ln2_tp[0], ln2_tp[1]]
            af32 = aT.bitcast(F32)
            lg_last = lg_mm[NT - 1] if SPARSE else 0
            ybank_free2 = [0, 0]
            y_rt_free = [rt_ps_free[0], rt_ps_free[1]]
            gy_store = [0, 0]

            for ex in range(NE):
                if ex == 0:
                    w1_wait = [("pe", xnT_lastread_pe)]
                    w2_wait_hi = [("pe", av_last), ("pe", sc_last)]
                    w2_wait_lo = [("pe", outproj_pe_done)]
                else:
                    w1_wait = [("pe", h_last_pe[ex - 1])]
                    w2_wait_hi = [("pe", y_last_pe[ex - 1])]
                    w2_wait_lo = [("pe", y_last_pe[ex - 1])]
                if SPARSE:
                    for fc in range(NFC):
                        w1_ld[ex] = p.em(
                            "sync",
                            (lambda ex, fc: lambda e: e.dma_start(
                                out=xnT[:, fc * 2048:(fc + 1) * 2048],
                                in_=d_w1b[ex, fc * 128:(fc + 1) * 128, :]))(ex, fc),
                            waits=(list(w1_wait) + [("pc", pc_total)]) if fc == 0 else (),
                            inc="gl1", amt=16)
                    b2x = (b2e, b2eB)[ex % 2]
                    b2w = () if ex < 2 else [("pe", y_last_pe[ex - 2])]
                    p.em("gpsimd",
                         (lambda ex, b2x: lambda e: e.dma_start(out=b2x[:, :],
                                                                in_=d_b2[0:1, ex * 1024:(ex + 1) * 1024]))(ex, b2x),
                         waits=b2w, inc="gl2", amt=16)
                    for fhc in range(NFH):
                        if fhc == 0:
                            w2w = list(w2_wait_lo) + [("pc", pc_total)]
                        elif fhc == 8:
                            w2w = w2_wait_hi
                        else:
                            w2w = ()
                        w2_ld[ex] = p.em(
                            "sync",
                            (lambda ex, fhc: lambda e: e.dma_start(
                                out=arB[:, fhc * 1024:(fhc + 1) * 1024],
                                in_=d_w2b[ex, fhc * 128:(fhc + 1) * 128, :]))(ex, fhc),
                            waits=w2w, inc="gl2", amt=16)
                else:
                    for fc in range(NFC):
                        w1_ld[ex] = p.em(
                            "gpsimd",
                            (lambda ex, fc: lambda e: e.dma_start(
                                out=xnT[:, fc * 2048:(fc + 1) * 2048],
                                in_=d_w1[ex, fc * 128:(fc + 1) * 128, :]))(ex, fc),
                            waits=w1_wait if fc == 0 else (), inc="gl1", amt=16)
                    p.em("gpsimd",
                         (lambda ex: lambda e: e.dma_start(out=b2e[:, :],
                                                           in_=d_b2[0:1, ex * 1024:(ex + 1) * 1024]))(ex),
                         waits=w2_wait_lo, inc="gl2", amt=16)
                    for fhc in range(NFH):
                        w2_ld[ex] = p.em(
                            "gpsimd",
                            (lambda ex, fhc: lambda e: e.dma_start(
                                out=arB[:, fhc * 1024:(fhc + 1) * 1024],
                                in_=d_w2[ex, fhc * 128:(fhc + 1) * 128, :]))(ex, fhc),
                            waits=(w2_wait_hi if fhc == 8 else ()),
                            inc="gl2", amt=16)

                if not SPARSE:
                    for qc in range(NQC):
                        re = [0] * NFH
                        for fhc in range(NFH):
                            slot = fhc % 4
                            ps, c0 = hb_aps[slot]
                            w = [("act", hbank_free[slot]), ("act", hbank_free_a[slot]),
                                 ("dve", hbank_free_d[slot]),
                                 ("gl1", w1_ld[ex]), ("dve", xn2_done[qc * 4 + 3]),
                                 ("pe", 0)]
                            for fc in range(NFC):
                                hm = p.em(
                                    "tensor",
                                    (lambda ps, c0, ex, qc, fhc, fc: lambda e: e.matmul(
                                        ps[:, c0:c0 + 512],
                                        xnT[:, fc * 2048 + fhc * 128: fc * 2048 + (fhc + 1) * 128],
                                        aT[:, fc * 2048 + qc * 512: fc * 2048 + (qc + 1) * 512],
                                        start=(fc == 0), stop=(fc == NFC - 1)))(ps, c0, ex, qc, fhc, fc),
                                    waits=(w if fc == 0 else ()), inc=("pe" if fc == NFC - 1 else None))
                            h_last_pe[ex] = hm
                            wre = [("pe", hm), ("ld", const_ld), ("pe", h_free_pe)]
                            re[fhc] = p.em(
                                "scalar",
                                (lambda ps, c0, ex, fhc: lambda e: e.activation(
                                    h_sb[:, fhc * 512:(fhc + 1) * 512], ps[:, c0:c0 + 512], AF.Relu,
                                    bias=b1t_s[:, ex * 16 + fhc:ex * 16 + fhc + 1],
                                    scale=consts[:, C_ONE:C_ONE + 1]))(ps, c0, ex, fhc),
                                waits=wre, inc="act")
                            hbank_free[slot] = re[fhc]
                            hbank_free_a[slot] = 0
                            hbank_free_d[slot] = 0
                        relu_f = p.em("scalar", act_fence(ffn=True), inc="act")
                        relu_last[ex] = relu_f
                        for tsub in range(4):
                            tt = qc * 4 + tsub
                            for fo in range(2):
                                b = (tsub * 2 + fo) % 2
                                ps = (psC, psD)[b]
                                w = [("act", relu_f), ("gl2", w2_ld[ex]),
                                     ("dve", ybank_free[b])]
                                p.em("tensor",
                                     (lambda ps, fo: lambda e: e.matmul(
                                         ps[:, 0:512], ones_bf[0:1, 0:128],
                                         b2e[0:1, fo * 512: fo * 512 + 512],
                                         start=True, stop=False))(ps, fo),
                                     waits=w)
                                for fhc in range(NFH):
                                    ym = p.em(
                                        "tensor",
                                        (lambda ps, fhc, tsub, fo: lambda e: e.matmul(
                                            ps[:, 0:512],
                                            h_sb[:, fhc * 512 + tsub * 128: fhc * 512 + (tsub + 1) * 128],
                                            arB[:, fhc * 1024 + fo * 512: fhc * 1024 + fo * 512 + 512],
                                            start=False, stop=(fhc == NFH - 1)))(ps, fhc, tsub, fo),
                                        inc=("pe" if fhc == NFH - 1 else None))
                                y_last_pe[ex] = ym
                                ybank_free[b] = x_final[tt][fo] = p.em(
                                    "vector",
                                    (lambda ps, tt, fo, ex: lambda e: e.scalar_tensor_tensor(
                                        x_sb[:, tt * 1024 + fo * 512: tt * 1024 + fo * 512 + 512],
                                        ps[:, 0:512], gate_sb[:, tt * 8 + ex: tt * 8 + ex + 1],
                                        x_sb[:, tt * 1024 + fo * 512: tt * 1024 + fo * 512 + 512],
                                        ALU.mult, ALU.add))(ps, tt, fo, ex),
                                    waits=[("pe", ym), ("dve", gb_x[tt])], inc="dve")
                        h_free_pe = y_last_pe[ex]
                else:
                    xfm = xn_tmp.bitcast(BF16)
                    for st in range(NST):
                        par = st % 2
                        gsem = "gliA" if par == 0 else "gliB"
                        base = ex * CAP + st * 128
                        gslice = (ex * NST + st) % 4
                        w = [("gliA", scatA), ("gliB", scatB), ("dve", gx_free[gslice])]
                        gx_c = p.em(
                            "gpsimd",
                            (lambda base, gslice: lambda e: e.dma_start(
                                out=gxb[:, gslice * 1024:(gslice + 1) * 1024],
                                in_=d_gxall[base:base + 128, :]))(base, gslice),
                            waits=w, inc=gsem, amt=16)
                        upw = [(gsem, gx_c), ("pe", af_free[par])]
                        if ex == 0 and st <= 1:
                            upw.append(("pe", lg_last))
                        upc = p.em(
                            "vector",
                            (lambda par, gslice: lambda e: e.tensor_copy(
                                af32[:, par * 1024:(par + 1) * 1024],
                                gxb[:, gslice * 1024:(gslice + 1) * 1024]))(par, gslice),
                            waits=upw, inc="dve")
                        gx_free[gslice] = upc
                        tp_cnt = [0] * (NFC + 1)
                        ev = [0] * NFC
                        for fc in range(NFC + 1):
                            bpar = fc % 2
                            tps = psE if bpar == 0 else psF
                            w = [(("act", "dve")[bpar], tp_bank_free[bpar]), ("ld", const_ld)]
                            if fc == 0:
                                w.append(("dve", upc))
                                if ex == 0 and st <= 1:
                                    w.append(("dve", flati_f))
                            src_col = (fc % NFC) * 128
                            tp_cnt[fc] = p.em(
                                "tensor",
                                (lambda tps, par, src_col: lambda e: e.transpose(
                                    tps[:, 0:128],
                                    af32[:, par * 1024 + src_col: par * 1024 + src_col + 128],
                                    identf[:, :]))(tps, par, src_col),
                                waits=w, inc="pe")
                            if fc >= 1:
                                pfc = fc - 1
                                pps = psE if pfc % 2 == 0 else psF
                                eng, snm = ("scalar", "act") if pfc % 2 == 0 else ("vector", "dve")
                                tp_bank_free[pfc % 2] = ev[pfc] = p.em(
                                    eng,
                                    (lambda pps, par, pfc: lambda e: (
                                        e.tensor_copy if hasattr(e, 'tensor_copy') else e.copy)(
                                        xfm[:, par * 1024 + pfc * 128: par * 1024 + (pfc + 1) * 128],
                                        pps[:, 0:128]))(pps, par, pfc),
                                    waits=[("pe", tp_cnt[pfc + 1])], inc=snm)
                        af_free[par] = tp_cnt[NFC]
                        ev_dve = ev[NFC - 1]
                        ev_act = ev[NFC - 2]
                        hst = [0] * NFH
                        for fhc in range(NFH):
                            slot = fhc % 4
                            ps, c0 = hb_aps[slot]
                            w = [("dve", ev_dve if fhc == 0 else 0),
                                 ("act", ev_act if fhc == 0 else 0),
                                 ("dve", hbank_free[slot]),
                                 ("act", hbank_free_a[slot]), ("dve", hbank_free_d[slot]),
                                 ("gl1", w1_ld[ex])]
                            for fc in range(NFC):
                                hm = p.em(
                                    "tensor",
                                    (lambda ps, c0, fhc, fc, par: lambda e: e.matmul(
                                        ps[:, c0:c0 + 128],
                                        xnT[:, fc * 2048 + fhc * 128: fc * 2048 + (fhc + 1) * 128],
                                        xfm[:, par * 1024 + fc * 128: par * 1024 + (fc + 1) * 128],
                                        start=(fc == 0), stop=(fc == NFC - 1)))(ps, c0, fhc, fc, par),
                                    waits=(w if fc == 0 else ()), inc=("pe" if fc == NFC - 1 else None))
                            h_last_pe[ex] = hm
                            hst[fhc] = hbank_free[slot] = p.em(
                                "vector",
                                (lambda ps, c0, ex, fhc, par: lambda e: e.scalar_tensor_tensor(
                                    h_sb[:, par * 2048 + fhc * 128: par * 2048 + (fhc + 1) * 128],
                                    ps[:, c0:c0 + 128],
                                    b1t_s[:, ex * 16 + fhc:ex * 16 + fhc + 1],
                                    z128f[:, :], ALU.add, ALU.max))(ps, c0, ex, fhc, par),
                                waits=[("pe", hm), ("pe", h_free_pe), ("ld", const_ld)], inc="dve")
                            hbank_free_a[slot] = 0
                            hbank_free_d[slot] = 0
                        relu_f = p.em("vector", dve_fence(), inc="dve")
                        for fo in range(2):
                            b = fo
                            ps = (psC, psD)[b]
                            w = [("dve", relu_f), ("gl2", w2_ld[ex]), ("act", ybank_free2[b]),
                                 ("dve", y_rt_free[b])]
                            p.em("tensor",
                                 (lambda ps, fo, b2x: lambda e: e.matmul(
                                     ps[:, 0:512], ones_bf[0:1, 0:128],
                                     b2x[0:1, fo * 512: fo * 512 + 512],
                                     start=True, stop=False))(ps, fo, b2x),
                                 waits=w)
                            for fhc in range(NFH):
                                ym = p.em(
                                    "tensor",
                                    (lambda ps, fhc, par, fo: lambda e: e.matmul(
                                        ps[:, 0:512],
                                        h_sb[:, par * 2048 + fhc * 128: par * 2048 + (fhc + 1) * 128],
                                        arB[:, fhc * 1024 + fo * 512: fhc * 1024 + fo * 512 + 512],
                                        start=False, stop=(fhc == NFH - 1)))(ps, fhc, par, fo),
                                    inc=("pe" if fhc == NFH - 1 else None))
                            y_last_pe[ex] = ym
                            ybank_free2[b] = p.em(
                                "scalar",
                                (lambda ps, par, fo: lambda e: e.copy(
                                    gyb[:, par * 1024 + fo * 512: par * 1024 + fo * 512 + 512],
                                    ps[:, 0:512]))(ps, par, fo),
                                waits=[("pe", ym), ("glxA" if par == 0 else "glxB", gy_store[par])],
                                inc="act")
                            y_rt_free[b] = 0
                        gy_store[par] = p.em(
                            "scalar",
                            (lambda base, par: lambda e: e.dma_start(
                                out=d_gy[base:base + 128, :],
                                in_=gyb[:, par * 1024:(par + 1) * 1024]))(base, par),
                            inc="glxA" if par == 0 else "glxB", amt=16)
                        h_free_pe = y_last_pe[ex]

            if SPARSE:
                gA_tot = p.c["glxA"]
                gB_tot = p.c["glxB"]
                g_free = [0, 0]
                for t in range(NT):
                    par = t % 2
                    gsem = "gliA" if par == 0 else "gliB"
                    w = [("glxA", gA_tot), ("glxB", gB_tot), ("dve", g_free[par]),
                         ("dve", flati_f), ("dve", max(gx_free))]
                    p.em("gpsimd",
                         (lambda t, par: lambda e: e.indirect_dma_start(
                             out=gxb[:, par * 1024:(par + 1) * 1024], out_offset=None,
                             in_=d_gy[:, :],
                             in_offset=bass.IndirectOffsetOnAxis(flati[:, t:t + 1], 0)))(t, par),
                         waits=w, inc=gsem, amt=16)
                    g2c = p.em(
                        "gpsimd",
                        (lambda t, par: lambda e: e.indirect_dma_start(
                            out=gyb[:, par * 1024:(par + 1) * 1024], out_offset=None,
                            in_=d_gy[:, :],
                            in_offset=bass.IndirectOffsetOnAxis(flati[:, 16 + t:17 + t], 0)))(t, par),
                        inc=gsem, amt=16)
                    p.em("vector",
                         (lambda t, par: lambda e: e.scalar_tensor_tensor(
                             x_sb[:, t * 1024:(t + 1) * 1024],
                             gxb[:, par * 1024:(par + 1) * 1024],
                             wstp[:, 800 + t:801 + t],
                             x_sb[:, t * 1024:(t + 1) * 1024], ALU.mult, ALU.add))(t, par),
                         waits=[(gsem, g2c)], inc="dve")
                    s2 = p.em(
                        "vector",
                        (lambda t, par: lambda e: e.scalar_tensor_tensor(
                            x_sb[:, t * 1024:(t + 1) * 1024],
                            gyb[:, par * 1024:(par + 1) * 1024],
                            wstp[:, 960 + t:961 + t],
                            x_sb[:, t * 1024:(t + 1) * 1024], ALU.mult, ALU.add))(t, par),
                        inc="dve")
                    g_free[par] = s2
                    x_final[t] = [s2, s2]

            st_f = p.em("vector", dve_fence(), inc="dve")
            for t in range(NT):
                p.em("sync",
                     (lambda t: lambda e: e.dma_start(out=d_out[t * 128:(t + 1) * 128, :],
                                                      in_=x_sb[:, t * 1024:(t + 1) * 1024]))(t),
                     waits=[("dve", x_final[t][0]), ("dve", x_final[t][1])], inc="st", amt=16)
            p.em("sync", lambda e: e.wait_ge(sems["st"], p.c["st"]))
            emit_block(nc, p, sems)

    return nc


def emit_block(nc, p, sems):
    with nc.Block() as block:
        def mk(name):
            def run(eng):
                seen = {}
                for fn, waits, inc, amt in p.streams[name]:
                    for s, v in waits:
                        if v > seen.get(s, 0):
                            eng.wait_ge(sems[s], v)
                            seen[s] = v
                    ins = fn(eng)
                    if inc is not None:
                        ins.then_inc(sems[inc], amt)
            return run
        block.sync(mk("sync"))
        block.gpsimd(mk("gpsimd"))
        block.tensor(mk("tensor"))
        block.scalar(mk("scalar"))
        block.vector(mk("vector"))


_CACHE = {}


def _get_nc(do_ffn=True, mode="full"):
    key = (do_ffn, mode)
    if key not in _CACHE:
        _CACHE[key] = build_nc(do_ffn=do_ffn, mode=mode)
    return _CACHE[key]


def kernel(L0_attn_x, L0_attn_Wq, L0_attn_Wk, L0_attn_Wv, L0_attn_W_out, L0_attn_b_out,
           L0_ffn_Wr, L0_ffn_br, L0_ffn_W1, L0_ffn_b1, L0_ffn_W2, L0_ffn_b2,
           _do_ffn=True, _mode="full"):
    x = np.ascontiguousarray(np.asarray(L0_attn_x, dtype=np.float32))
    wq = np.ascontiguousarray(np.asarray(L0_attn_Wq, dtype=np.float32))
    wk = np.ascontiguousarray(np.asarray(L0_attn_Wk, dtype=np.float32))
    wvt = np.ascontiguousarray(np.asarray(L0_attn_Wv, dtype=np.float32).T)
    woutt = np.ascontiguousarray(np.asarray(L0_attn_W_out, dtype=np.float32).T)
    bout = np.ascontiguousarray(np.asarray(L0_attn_b_out, dtype=np.float32).reshape(1, E))
    wr = np.ascontiguousarray(np.asarray(L0_ffn_Wr, dtype=np.float32))
    br = np.ascontiguousarray(np.asarray(L0_ffn_br, dtype=np.float32).reshape(1, NE))
    w1 = np.ascontiguousarray(np.asarray(L0_ffn_W1, dtype=np.float32))
    b1t = np.ascontiguousarray(np.asarray(L0_ffn_b1, dtype=np.float32).reshape(NE, NFH, 128).transpose(0, 2, 1))
    w2 = np.ascontiguousarray(np.asarray(L0_ffn_W2, dtype=np.float32))
    b2 = np.ascontiguousarray(np.asarray(L0_ffn_b2, dtype=np.float32).reshape(1, NE * E))

    import ml_dtypes
    cst = np.zeros((128, 16), np.float32)
    for col, val in ((0, 1.0 / 32.0), (1, -EXPC), (2, 1e-5), (3, 1.0 / 1024.0),
                     (4, -0.5), (5, 1.0), (6, -1.0), (7, -1e30), (8, float(CAP)),
                     (9, 8192.0), (10, float(NE * CAP))):
        cst[:, col] = val
    identb = np.eye(128, dtype=ml_dtypes.bfloat16)
    identf = np.eye(128, dtype=np.float32)
    onesb = np.ones((1, 128), ml_dtypes.bfloat16)
    onesf = np.ones((65, 64), np.float32)
    zb = np.zeros((64, 128), ml_dtypes.bfloat16)
    zbig = np.zeros((128, E), ml_dtypes.bfloat16)
    onescol = np.ones((128, 1), ml_dtypes.bfloat16)
    zf = np.zeros((128, 1120), np.float32)
    stri = np.triu(np.ones((128, 128), np.float32), 1)
    ebase = np.zeros((128, 128), np.float32)
    for t_ in range(16):
        for e_ in range(8):
            ebase[:, t_ * 8 + e_] = e_ * CAP


    nc = _get_nc(do_ffn=_do_ffn, mode=_mode)
    in_maps = []
    for c in range(8):
        in_maps.append({
            "x": np.ascontiguousarray(x[c]),
            "wq": wq, "wk": wk, "wvt": wvt, "woutt": woutt, "bout": bout,
            "wr": wr, "br": br, "w1": w1, "b1t": b1t, "w2": w2, "b2": b2,
            "cst": cst, "identb": identb, "identf": identf, "onesb": onesb, "onesf": onesf,
            "zb": zb, "onescol": onescol, "zf": zf, "stri": stri, "ebase": ebase,
            "zbig": zbig,
        })
    res = run_bass_kernel_spmd(nc, in_maps, core_ids=list(range(8)))
    if _mode == "rt":
        keys = ("out", "dbgf", "dbgl", "dbgw", "dbgi", "dbgx", "xn2_d", "gy_d", "gxall_d")
        return {k: np.stack([np.asarray(res.results[c][k]) for c in range(8)], axis=0)
                for k in keys}
    return np.stack([np.asarray(res.results[c]["out"]) for c in range(8)], axis=0)


# version 10
# speedup vs baseline: 1.0067x; 1.0067x over previous
import numpy as np
from contextlib import ExitStack
import concourse.bass as bass
import concourse.mybir as mybir
from concourse.bass_utils import run_bass_kernel_spmd

F32 = mybir.dt.float32
BF16 = mybir.dt.bfloat16
I32 = mybir.dt.int32
AF = mybir.ActivationFunctionType
ALU = mybir.AluOpType
AX = mybir.AxisListType

T = 2048
E = 1024
NT = 16
NFC = 8
NKT = 16
NQC = 4
QW = 512
NE = 8
FH = 2048
NFH = 16

WOUT0 = 0
YP0 = 8192
VTA0 = 10240
VTB0 = 11280
PT0 = 12320
PT1 = 13344

C_ESC = 0
C_EBI = 1
C_EPS = 2
C_INV = 3
C_NH = 4
C_ONE = 5
C_NEG = 6
C_BIG = 7
C_CAP = 8
C_HG = 9
C_TR = 10
EXPC = 12.0
SPARSE = True
CAP = 768
NST = CAP // 128


class Prog:
    def __init__(self):
        self.streams = {n: [] for n in ("sync", "gpsimd", "tensor", "scalar", "vector")}
        self.c = {"ld": 0, "gs": 0, "gl1": 0, "gl2": 0, "glx": 0, "glxA": 0, "glxB": 0,
                  "gliA": 0, "gliB": 0, "fl": 0, "pc": 0, "st": 0, "pe": 0, "act": 0, "dve": 0}

    def em(self, stream, fn, waits=(), inc=None, amt=1):
        ws = tuple((s, v) for (s, v) in waits if v and v > 0)
        self.streams[stream].append((fn, ws, inc, amt))
        if inc is None:
            return None
        self.c[inc] += amt
        return self.c[inc]


def build_nc(do_ffn=True, mode="full"):
    nc = bass.Bass("TRN2", target_bir_lowering=False, detect_race_conditions=False)
    if not do_ffn:
        mode = "attn"
    d_dbg1 = d_dbg2 = None
    if mode in ("attn", "pair0"):
        d_dbga = nc.dram_tensor("dbga", [128, 16384], BF16, kind="ExternalOutput").ap()
    if mode == "pair0":
        d_dbgb = nc.dram_tensor("dbgb", [128, 16384], BF16, kind="ExternalOutput").ap()
        d_dbgr = nc.dram_tensor("dbgr", [128, 144], F32, kind="ExternalOutput").ap()
        d_dbgs = nc.dram_tensor("dbgs", [128, 64], F32, kind="ExternalOutput").ap()
    if mode == "ln":
        d_dbg1 = nc.dram_tensor("dbg1", [128, 16384], BF16, kind="ExternalOutput").ap()
        d_dbg2 = nc.dram_tensor("dbg2", [128, 144], F32, kind="ExternalOutput").ap()
        d_dbg3 = nc.dram_tensor("dbg3", [128, 112], F32, kind="ExternalOutput").ap()
    d_x = nc.dram_tensor("x", [T, E], F32, kind="ExternalInput").ap()
    d_wq = nc.dram_tensor("wq", [64, 64], F32, kind="ExternalInput").ap()
    d_wk = nc.dram_tensor("wk", [64, 64], F32, kind="ExternalInput").ap()
    d_wvt = nc.dram_tensor("wvt", [64, 64], F32, kind="ExternalInput").ap()
    d_woutt = nc.dram_tensor("woutt", [E, E], F32, kind="ExternalInput").ap()
    d_bout = nc.dram_tensor("bout", [1, E], F32, kind="ExternalInput").ap()
    d_wr = nc.dram_tensor("wr", [E, NE], F32, kind="ExternalInput").ap()
    d_br = nc.dram_tensor("br", [1, NE], F32, kind="ExternalInput").ap()
    d_w1 = nc.dram_tensor("w1", [NE, E, FH], F32, kind="ExternalInput").ap()
    d_b1t = nc.dram_tensor("b1t", [NE, 128, NFH], F32, kind="ExternalInput").ap()
    d_w2 = nc.dram_tensor("w2", [NE, FH, E], F32, kind="ExternalInput").ap()
    d_b2 = nc.dram_tensor("b2", [1, NE * E], F32, kind="ExternalInput").ap()
    d_cst = nc.dram_tensor("cst", [128, 16], F32, kind="ExternalInput").ap()
    d_ident = nc.dram_tensor("identb", [128, 128], BF16, kind="ExternalInput").ap()
    d_identf = nc.dram_tensor("identf", [128, 128], F32, kind="ExternalInput").ap()
    d_onesb = nc.dram_tensor("onesb", [1, 128], BF16, kind="ExternalInput").ap()
    d_onesf = nc.dram_tensor("onesf", [65, 64], F32, kind="ExternalInput").ap()
    d_zb = nc.dram_tensor("zb", [64, 128], BF16, kind="ExternalInput").ap()
    d_onescol = nc.dram_tensor("onescol", [128, 1], BF16, kind="ExternalInput").ap()
    d_zf = nc.dram_tensor("zf", [128, 1120], F32, kind="ExternalInput").ap()
    d_stri = nc.dram_tensor("stri", [128, 128], F32, kind="ExternalInput").ap()
    d_ebase = nc.dram_tensor("ebase", [128, 128], F32, kind="ExternalInput").ap()
    ik = "ExternalOutput" if mode == "rt" else "Internal"
    d_xn2 = nc.dram_tensor("xn2_d", [T, E], BF16, kind=ik).ap()
    d_gy = nc.dram_tensor("gy_d", [NE * CAP + 128, E], BF16, kind=ik).ap()
    d_gxall = nc.dram_tensor("gxall_d", [NE * CAP + 128, E], BF16, kind=ik).ap()
    if mode == "rt":
        d_dbgf = nc.dram_tensor("dbgf", [128, 128], I32, kind="ExternalOutput").ap()
        d_dbgl = nc.dram_tensor("dbgl", [128, 128], F32, kind="ExternalOutput").ap()
        d_dbgw = nc.dram_tensor("dbgw", [128, 1120], F32, kind="ExternalOutput").ap()
        d_dbgi = nc.dram_tensor("dbgi", [128, 256], F32, kind="ExternalOutput").ap()
        d_dbgx = nc.dram_tensor("dbgx", [128, 1120], F32, kind="ExternalOutput").ap()
    d_zbig = nc.dram_tensor("zbig", [128, E], BF16, kind="ExternalInput").ap()
    d_w1b = nc.dram_tensor("w1b_d", [NE, E, FH], BF16, kind="Internal").ap()
    d_w2b = nc.dram_tensor("w2b_d", [NE, FH, E], BF16, kind="Internal").ap()
    d_out = nc.dram_tensor("out", [T, E], F32, kind="ExternalOutput").ap()

    p = Prog()
    es = ExitStack()
    with es:
        ec = es.enter_context
        x_sb = ec(nc.sbuf_tensor("x_sb", [128, NT * 1024], F32))
        xnT = ec(nc.sbuf_tensor("xnT", [128, 16384], BF16))
        aT = ec(nc.sbuf_tensor("aT", [128, 16384], BF16))
        arB = ec(nc.sbuf_tensor("arB", [128, 16384], BF16))
        h_sb = ec(nc.sbuf_tensor("h_sb", [128, 8192], BF16))
        xn_tmp = ec(nc.sbuf_tensor("xn_tmp", [128, 1024], F32))
        st1 = ec(nc.sbuf_tensor("st1", [128, 352], F32))
        lgA = ec(nc.sbuf_tensor("lgA", [128, 128], F32))
        gate_sb = None if SPARSE else ec(nc.sbuf_tensor("gate_sb", [128, 128], F32))
        lgM = ec(nc.sbuf_tensor("lgM", [128, 128], F32))
        is1t = ec(nc.sbuf_tensor("is1t", [128, 128], F32))
        is2t = ec(nc.sbuf_tensor("is2t", [128, 128], F32))
        consts = ec(nc.sbuf_tensor("consts", [128, 16], F32))
        ones_bf = ec(nc.sbuf_tensor("ones_bf", [1, 128], BF16))
        ones_f = ec(nc.sbuf_tensor("ones_f", [65, 64], F32))
        ident = ec(nc.sbuf_tensor("ident", [128, 128], BF16))
        identf = ec(nc.sbuf_tensor("identf_s", [128, 128], F32))
        Gp = ec(nc.sbuf_tensor("Gp", [128, 128], BF16))
        WvTp = ec(nc.sbuf_tensor("WvTp", [128, 128], BF16))
        wqb = ec(nc.sbuf_tensor("wqb", [128, 64], BF16))
        wkb = ec(nc.sbuf_tensor("wkb", [128, 64], BF16))
        Wr_bf = ec(nc.sbuf_tensor("Wr_bf", [128, 64], BF16))
        br_bf = ec(nc.sbuf_tensor("br_bf", [1, 8], BF16))
        bo_bf = ec(nc.sbuf_tensor("bo_bf", [1, 1024], BF16))
        b2e = ec(nc.sbuf_tensor("b2e", [1, 1024], BF16))
        b2eB = ec(nc.sbuf_tensor("b2eB", [1, 1024], BF16))
        b1t_s = ec(nc.sbuf_tensor("b1t_s", [128, 128], F32))
        fscr = ec(nc.sbuf_tensor("fscr", [128, 4], F32))
        stri = ec(nc.sbuf_tensor("stri_s", [128, 128], F32))
        ebase = ec(nc.sbuf_tensor("ebase_s", [128, 128], F32))
        flati = ec(nc.sbuf_tensor("flati", [128, 128], I32))
        gxb = ec(nc.sbuf_tensor("gxb", [128, 2048], BF16))
        gyb = ec(nc.sbuf_tensor("gyb", [128, 2048], BF16))
        z128f = ec(nc.sbuf_tensor("z128f", [128, 128], F32))

        psA = ec(nc.psum_tensor("psA", [128, 1024], F32))
        psB = ec(nc.psum_tensor("psB", [128, 1024], F32))
        psC = ec(nc.psum_tensor("psC", [128, 512], F32))
        psD = ec(nc.psum_tensor("psD", [128, 512], F32))
        psE = ec(nc.psum_tensor("psE", [128, 512], F32))
        psF = ec(nc.psum_tensor("psF", [128, 512], F32))

        sems = {}
        for name in ("ld", "gs", "gl1", "gl2", "glx", "glxA", "glxB", "gliA", "gliB", "fl", "pc", "st", "pe", "act", "dve"):
            sems[name] = ec(nc.semaphore("s_" + name))

        stx = h_sb[:, 2048:4288].bitcast(F32)
        st2 = st1
        rr = stx[:, 0:1024]
        wstp = h_sb[:, 4288:6528].bitcast(F32)
        FD_B = h_sb[:, 6528:7552]

        def dve_fence(dest=None):
            d = lgA if dest is None else dest
            return lambda e: e.tensor_add(d[:, 0:128], x_sb[:, 0:128], x_sb[:, 128:256])

        def act_fence(ffn=False):
            if ffn:
                return lambda e: e.copy(lgM[:, 0:128], x_sb[:, 0:128])
            return lambda e: e.copy(FD_B[:, 512:1024], x_sb[:, 0:512])

        x_ld = []
        for t in range(NT):
            x_ld.append(p.em(
                "sync",
                (lambda t: lambda e: e.dma_start(out=x_sb[:, t * 1024:(t + 1) * 1024],
                                                 in_=d_x[t * 128:(t + 1) * 128, :]))(t),
                inc="ld", amt=16))
        qkv_ld = p.c["ld"]
        for ex in range(NE):
            p.em("sync", (lambda ex: lambda e: e.dma_start(out=b1t_s[:, ex * 16:(ex + 1) * 16],
                                                           in_=d_b1t[ex, :, :]))(ex),
                 inc="ld", amt=16)

        wout_ld = None
        for fc in range(NFC):
            wout_ld = p.em(
                "gpsimd",
                (lambda fc: lambda e: e.dma_start(out=arB[:, WOUT0 + fc * 1024: WOUT0 + (fc + 1) * 1024],
                                                  in_=d_woutt[fc * 128:(fc + 1) * 128, :]))(fc),
                inc="gs", amt=16)
        for fc in range(NFC):
            wr_ld = p.em(
                "gpsimd",
                (lambda fc: lambda e: e.dma_start(out=Wr_bf[:, fc * 8:(fc + 1) * 8],
                                                  in_=d_wr[fc * 128:(fc + 1) * 128, :]))(fc),
                inc="gs", amt=16)
        br_ld = p.em("gpsimd", lambda e: e.dma_start(out=br_bf[:, :], in_=d_br[:, :]), inc="gs", amt=16)
        bo_ld = p.em("gpsimd", lambda e: e.dma_start(out=bo_bf[:, :], in_=d_bout[:, :]), inc="gs", amt=16)
        b2_ld = 0
        gs_total = p.c["gs"]
        wv_gl = wout_ld = wr_ld = br_ld = bo_ld = b2_ld = gs_total
        p.em("gpsimd", lambda e: e.dma_start(out=wqb[0:64, :], in_=d_wq[:, :]), inc="gs", amt=16)
        p.em("gpsimd", lambda e: e.dma_start(out=wqb[64:128, :], in_=d_wq[:, :]), inc="gs", amt=16)
        p.em("gpsimd", lambda e: e.dma_start(out=wkb[0:64, :], in_=d_wk[:, :]), inc="gs", amt=16)
        p.em("gpsimd", lambda e: e.dma_start(out=wkb[64:128, :], in_=d_wk[:, :]), inc="gs", amt=16)
        p.em("gpsimd", lambda e: e.dma_start(out=WvTp[0:64, 0:64], in_=d_wvt[:, :]), inc="gs", amt=16)
        wv_gl = p.em("gpsimd", lambda e: e.dma_start(out=WvTp[64:128, 64:128], in_=d_wvt[:, :]),
                     inc="gs", amt=16)

        p.em("sync", lambda e: e.dma_start(out=consts[:, :], in_=d_cst[:, :]), inc="ld", amt=16)
        p.em("sync", lambda e: e.dma_start(out=ident[:, :], in_=d_ident[:, :]), inc="ld", amt=16)
        p.em("sync", lambda e: e.dma_start(out=identf[:, :], in_=d_identf[:, :]), inc="ld", amt=16)
        p.em("sync", lambda e: e.dma_start(out=ones_bf[:, :], in_=d_onesb[:, :]), inc="ld", amt=16)
        p.em("sync", lambda e: e.dma_start(out=ones_f[:, :], in_=d_onesf[:, :]), inc="ld", amt=16)
        p.em("sync", lambda e: e.dma_start(out=Gp[0:64, 64:128], in_=d_zb[:, 0:64]), inc="ld", amt=16)
        p.em("sync", lambda e: e.dma_start(out=Gp[64:128, 0:64], in_=d_zb[:, 0:64]), inc="ld", amt=16)
        p.em("sync", lambda e: e.dma_start(out=WvTp[0:64, 64:128], in_=d_zb[:, 0:64]), inc="ld", amt=16)
        p.em("sync", lambda e: e.dma_start(out=WvTp[64:128, 0:64], in_=d_zb[:, 0:64]), inc="ld", amt=16)
        for kt in range(NKT):
            p.em("sync", (lambda kt: lambda e: e.dma_start(
                out=arB[:, VTA0 + kt * 65 + 64: VTA0 + kt * 65 + 65], in_=d_onescol[:, :]))(kt),
                 inc="ld", amt=16)
            p.em("sync", (lambda kt: lambda e: e.dma_start(
                out=arB[:, VTB0 + kt * 65 + 64: VTB0 + kt * 65 + 65], in_=d_onescol[:, :]))(kt),
                 inc="ld", amt=16)
        p.em("sync", lambda e: e.dma_start(out=st1[:, :], in_=d_zf[:, 0:352]), inc="ld", amt=16)
        p.em("sync", lambda e: e.dma_start(out=wstp[:, :], in_=d_zf[:, :]), inc="ld", amt=16)
        p.em("sync", lambda e: e.dma_start(out=stri[:, :], in_=d_stri[:, :]), inc="ld", amt=16)
        p.em("sync", lambda e: e.dma_start(out=ebase[:, :], in_=d_ebase[:, :]), inc="ld", amt=16)
        p.em("sync", lambda e: e.dma_start(out=z128f[:, :], in_=d_zf[:, 0:128]), inc="ld", amt=16)
        const_ld = p.c["ld"]
        pc_total = 0
        if SPARSE and do_ffn:
            for ex in range(NE):
                for fc in range(NFC):
                    pc_total = p.em(
                        "gpsimd",
                        (lambda ex, fc: lambda e: e.dma_start(
                            out=d_w1b[ex, fc * 128:(fc + 1) * 128, :],
                            in_=d_w1[ex, fc * 128:(fc + 1) * 128, :]))(ex, fc),
                        waits=([("ld", const_ld)] if pc_total == 0 else ()),
                        inc="pc", amt=16)
                for fhc in range(NFH):
                    pc_total = p.em(
                        "gpsimd",
                        (lambda ex, fhc: lambda e: e.dma_start(
                            out=d_w2b[ex, fhc * 128:(fhc + 1) * 128, :],
                            in_=d_w2[ex, fhc * 128:(fhc + 1) * 128, :]))(ex, fhc),
                        inc="pc", amt=16)
        fl_total = 0
        if SPARSE:
            for i in range(NE * NST + 1):
                fl_total = p.em(
                    "sync",
                    (lambda i: lambda e: e.dma_start(out=d_gxall[i * 128:(i + 1) * 128, :],
                                                     in_=d_zbig[:, :]))(i),
                    inc="fl", amt=16)
            fl_total = p.em(
                "sync", lambda e: e.dma_start(out=d_gy[NE * CAP:NE * CAP + 128, :],
                                              in_=d_zbig[:, :]), inc="fl", amt=16)
        casts_done = 0

        p.em("tensor", lambda e: e.matmul(psC[0:64, 0:64], wkb[0:64, :], wqb[0:64, :],
                                          start=True, stop=True, tile_position=(0, 0)),
             waits=[("gs", wv_gl), ("ld", const_ld)])
        g_mm = p.em("tensor", lambda e: e.matmul(psC[64:128, 64:128], wkb[64:128, :], wqb[64:128, :],
                                                 start=True, stop=True, tile_position=(64, 64)), inc="pe")
        p.em("scalar", lambda e: e.copy(Gp[0:64, 0:64], psC[0:64, 0:64]), waits=[("pe", g_mm)])
        g_done = p.em("scalar", lambda e: e.copy(Gp[64:128, 64:128], psC[64:128, 64:128]), inc="act")
        psC_free_act = g_done
        psC_free_dve = 0
        psD_free_act = 0
        psD_free_dve = 0

        def emit_ln(stats, x_wait_sem, x_waits, dst_T, pe_extra_wait, dump_xn2=False):
            for t in range(NT):
                p.em("vector",
                     (lambda t: lambda e: e.tensor_mul(h_sb[:, 0:1024],
                                                       x_sb[:, t * 1024:(t + 1) * 1024],
                                                       x_sb[:, t * 1024:(t + 1) * 1024]))(t),
                     waits=[(x_wait_sem, x_waits[t]), ("ld", const_ld)])
                p.em("vector",
                     (lambda t: lambda e: e.reduce_sum(stats[:, t:t + 1], h_sb[:, 0:1024], AX.X))(t))
                p.em("vector",
                     (lambda t: lambda e: e.reduce_sum(stats[:, 176 + t:177 + t],
                                                       x_sb[:, t * 1024:(t + 1) * 1024], AX.X))(t))
            p.em("vector", lambda e: e.tensor_scalar(stx[:, 0:160], stats[:, 176:336],
                                                     consts[:, C_INV:C_INV + 1], None, ALU.mult),
                 waits=[("ld", const_ld)])
            p.em("vector", lambda e: e.tensor_scalar(stx[:, 160:320], stats[:, 0:160],
                                                     consts[:, C_INV:C_INV + 1], None, ALU.mult))
            p.em("vector", lambda e: e.tensor_mul(stx[:, 320:480], stx[:, 0:160], stx[:, 0:160]))
            p.em("vector", lambda e: e.tensor_sub(stx[:, 480:640], stx[:, 160:320], stx[:, 320:480]))
            var_done = p.em("vector", dve_fence(), inc="dve")
            p.em("scalar", lambda e: e.activation(stx[:, 640:800], stx[:, 480:640], AF.Ln,
                                                  bias=consts[:, C_EPS:C_EPS + 1]),
                 waits=[("dve", var_done), ("ld", const_ld)])
            p.em("scalar", lambda e: e.activation(stx[:, 800:960], stx[:, 640:800], AF.Exp,
                                                  scale=consts[:, C_NH:C_NH + 1]))
            rstd_f = p.em("scalar", act_fence(), inc="act")
            p.em("vector", lambda e: e.scalar_tensor_tensor(stx[:, 960:1120], stx[:, 0:160],
                                                            consts[:, C_NEG:C_NEG + 1],
                                                            stx[:, 800:960], ALU.mult, ALU.mult),
                 waits=[("act", rstd_f)])
            nb_done = p.em("vector", dve_fence(), inc="dve")
            evac = [[0] * NFC for _ in range(NT)]
            norm = [0] * NT
            tp_prev = [0, 0]
            last_tp = 0
            for t in range(NT):
                w = [("dve", nb_done)]
                if t > 0:
                    w.append(("pe", last_tp))
                    if dump_xn2:
                        w.append(("glx", 16 * t))
                p.em("scalar",
                     (lambda t: lambda e: e.activation(xn_tmp[:, :], x_sb[:, t * 1024:(t + 1) * 1024],
                                                       AF.Identity, bias=stx[:, 960 + t:961 + t],
                                                       scale=stx[:, 800 + t:801 + t]))(t),
                     waits=w)
                norm[t] = p.em("scalar", act_fence(), inc="act")
                if dump_xn2:
                    p.em("gpsimd",
                         (lambda t: lambda e: e.dma_start(out=d_xn2[t * 128:(t + 1) * 128, :],
                                                          in_=xn_tmp[:, :]))(t),
                         waits=[("act", norm[t])], inc="glx", amt=16)
                tp_cnt = [0] * (NFC + 1)
                for fc in range(NFC + 1):
                    par = fc % 2
                    ps = psE if par == 0 else psF
                    w = [("act", norm[t]), (("act", "dve")[par], tp_prev[par]), ("ld", const_ld)]
                    if t == 0 and fc == 0 and pe_extra_wait is not None:
                        w.append(pe_extra_wait)
                    src_col = (fc % NFC) * 128
                    tp_cnt[fc] = last_tp = p.em(
                        "tensor",
                        (lambda ps, src_col: lambda e: e.transpose(
                            ps[:, 0:128], xn_tmp[:, src_col:src_col + 128], identf[:, :]))(ps, src_col),
                        waits=w, inc="pe")
                    if fc >= 1:
                        pfc = fc - 1
                        pps = psE if pfc % 2 == 0 else psF
                        eng, snm = ("scalar", "act") if pfc % 2 == 0 else ("vector", "dve")
                        tp_prev[pfc % 2] = evac[t][pfc] = p.em(
                            eng,
                            (lambda pps, t, pfc: lambda e: (e.tensor_copy if hasattr(e, 'tensor_copy') else e.copy)(
                                dst_T[:, pfc * 2048 + t * 128: pfc * 2048 + (t + 1) * 128],
                                pps[:, 0:128]))(pps, t, pfc),
                            waits=[("pe", tp_cnt[pfc + 1])], inc=snm)
            fin = p.em("vector", dve_fence(), inc="dve")
            evac[NT - 1][NFC - 1] = fin
            return evac, norm, tp_prev

        ln1_evac, _, _ = emit_ln(st1, "ld", [const_ld] * NT, xnT, None)
        xnT_done = ln1_evac[NT - 1][NFC - 1]

        if mode == "ln":
            for fc in range(NFC):
                p.em("sync",
                     (lambda fc: lambda e: e.dma_start(out=d_dbg1[:, fc * 2048:(fc + 1) * 2048],
                                                       in_=xnT[:, fc * 2048:(fc + 1) * 2048]))(fc),
                     waits=[("dve", xnT_done)], inc="st", amt=16)
            p.em("sync", lambda e: e.dma_start(out=d_dbg2[:, :], in_=st1[:, :]),
                 waits=[("dve", xnT_done)], inc="st", amt=16)
            p.em("sync", lambda e: e.dma_start(out=d_dbg3[:, :], in_=wst[:, :]),
                 waits=[("dve", xnT_done)], inc="st", amt=16)
            p.em("sync", lambda e: e.wait_ge(sems["st"], p.c["st"]))
            emit_block(nc, p, sems)
            return nc

        bankA_free_act = 0
        bankB_free_act = 0
        yp_free = 0
        vt_free = 0
        pt_free = [0, 0]

        sc_last = 0
        av_last = 0
        rr_lastread = [0, 0]
        oa_mul_free = [0, 0]
        tmp_mul_free = [0, 0]
        pending = None
        mul_counts = []

        def emit_epilogue(ctx):
            nonlocal rr_lastread, mul_counts
            hp, qc, par, oA, oB, av15 = ctx
            colA = hp * 2048 + qc * 512
            r0 = 64 * par
            evA = p.em("scalar",
                       (lambda oA, r0: lambda e: e.copy(xn_tmp[r0:r0 + 64, 0:512], oA[0:64, 0:512]))(oA, r0),
                       waits=[("pe", av15), ("dve", tmp_mul_free[par])], inc="act")
            evB = p.em("scalar",
                       (lambda oB, r0: lambda e: e.copy(xn_tmp[r0:r0 + 64, 512:1024], oB[0:64, 0:512]))(oB, r0),
                       inc="act")
            rcA = p.em("vector",
                       (lambda oA: lambda e: e.reciprocal(rr[64:65, 0:512], oA[64:65, 0:512]))(oA),
                       waits=[("pe", av15), ("pe", rr_lastread[0])], inc="dve")
            rcB = p.em("vector",
                       (lambda oB: lambda e: e.reciprocal(rr[64:65, 512:1024], oB[64:65, 0:512]))(oB),
                       waits=[("pe", rr_lastread[1])], inc="dve")
            rbA = p.em("tensor",
                       (lambda oA: lambda e: e.matmul(oA[0:64, 0:512], ones_f[64:65, 0:64],
                                                      rr[64:65, 0:512], start=True, stop=True,
                                                      skip_group_check=True))(oA),
                       waits=[("act", evA), ("dve", rcA), ("ld", const_ld)], inc="pe")
            rbB = p.em("tensor",
                       (lambda oB: lambda e: e.matmul(oB[0:64, 0:512], ones_f[64:65, 0:64],
                                                      rr[64:65, 512:1024], start=True, stop=True,
                                                      skip_group_check=True))(oB),
                       waits=[("dve", rcB)], inc="pe")
            rr_lastread = [rbA, rbB]
            muA = p.em("vector",
                       (lambda colA, oA, r0: lambda e: e.tensor_mul(
                           aT[0:64, colA:colA + 512], xn_tmp[r0:r0 + 64, 0:512],
                           oA[0:64, 0:512]))(colA, oA, r0),
                       waits=[("pe", rbA), ("act", evB)], inc="dve")
            muB = p.em("vector",
                       (lambda colA, oB, r0: lambda e: e.tensor_mul(
                           aT[64:128, colA:colA + 512], xn_tmp[r0:r0 + 64, 512:1024],
                           oB[0:64, 0:512]))(colA, oB, r0),
                       waits=[("pe", rbB)], inc="dve")
            mul_counts.append((muA, muB))
            oa_mul_free[par] = muB
            tmp_mul_free[par] = muB

        gqc = 0
        for hp in range(1 if mode == 'pair0' else NFC):
            ym = []
            for i in range(4):
                ps = (psA, psB)[i // 2]
                col = (i % 2) * 512
                w = [("act", g_done), ("dve", xnT_done),
                     ("act", bankA_free_act if i // 2 == 0 else bankB_free_act)]
                ym.append(p.em(
                    "tensor",
                    (lambda ps, col, hp, i: lambda e: e.matmul(
                        ps[:, col:col + 512], Gp[:, :],
                        xnT[:, hp * 2048 + i * 512: hp * 2048 + (i + 1) * 512],
                        start=True, stop=True))(ps, col, hp, i),
                    waits=w, inc="pe"))
            ye = []
            for i in range(4):
                ps = (psA, psB)[i // 2]
                col = (i % 2) * 512
                ye.append(p.em(
                    "scalar",
                    (lambda ps, col, i: lambda e: e.copy(
                        arB[:, YP0 + i * 512: YP0 + (i + 1) * 512], ps[:, col:col + 512]))(ps, col, i),
                    waits=[("pe", ym[i]), ("pe", yp_free)], inc="act"))
            bankA_free_act = ye[1]
            bankB_free_act = ye[3]

            ve_prev = [bankA_free_act, bankB_free_act]
            ve_last = 0
            for tt in range(NT):
                ps = (psA, psB)[tt % 2]
                w = [("gs", wv_gl), ("ld", const_ld), ("dve", xnT_done),
                     ("act", ve_prev[tt % 2]), ("pe", vt_free)]
                vm = p.em(
                    "tensor",
                    (lambda ps, hp, tt: lambda e: e.matmul(
                        ps[:, 0:128], xnT[:, hp * 2048 + tt * 128: hp * 2048 + (tt + 1) * 128],
                        WvTp[:, :], start=True, stop=True))(ps, hp, tt),
                    waits=w, inc="pe")
                p.em("scalar",
                     (lambda ps, tt: lambda e: e.copy(arB[:, VTA0 + tt * 65: VTA0 + tt * 65 + 64],
                                                      ps[:, 0:64]))(ps, tt),
                     waits=[("pe", vm)], inc="act")
                ve_last = p.em(
                    "scalar",
                    (lambda ps, tt: lambda e: e.copy(arB[:, VTB0 + tt * 65: VTB0 + tt * 65 + 64],
                                                     ps[:, 64:128]))(ps, tt),
                    inc="act")
                ve_prev[tt % 2] = ve_last
            bankA_free_act = ve_prev[0]
            bankB_free_act = ve_prev[1]

            for qc in range(NQC):
                par = gqc % 2
                oA, oB = ((psC, psD), (psE, psF))[par]
                sc = [0] * NKT
                expd = [0] * NKT
                avd = [0] * NKT

                def emit_scores(kt):
                    nonlocal bankA_free_act, bankB_free_act, sc_last
                    ps = (psA, psB)[kt % 2]
                    bfree = bankA_free_act if kt % 2 == 0 else bankB_free_act
                    w = [("act", bfree), ("act", ve_last), ("act", ye[3])]
                    p.em("tensor",
                         (lambda ps, hp, kt, qc: lambda e: e.matmul(
                             ps[:, 0:512], arB[0:64, YP0 + kt * 128: YP0 + (kt + 1) * 128],
                             xnT[0:64, hp * 2048 + qc * 512: hp * 2048 + (qc + 1) * 512],
                             start=True, stop=True, tile_position=(0, 0)))(ps, hp, kt, qc),
                         waits=w)
                    sc[kt] = sc_last = p.em(
                        "tensor",
                        (lambda ps, hp, kt, qc: lambda e: e.matmul(
                            ps[:, 512:1024], arB[64:128, YP0 + kt * 128: YP0 + (kt + 1) * 128],
                            xnT[64:128, hp * 2048 + qc * 512: hp * 2048 + (qc + 1) * 512],
                            start=True, stop=True, tile_position=(64, 0)))(ps, hp, kt, qc),
                        inc="pe")

                def emit_exp(kt):
                    nonlocal bankA_free_act, bankB_free_act
                    ps = (psA, psB)[kt % 2]
                    pt_col = PT0 if kt % 2 == 0 else PT1
                    expd[kt] = p.em(
                        "scalar",
                        (lambda ps, pt_col: lambda e: e.activation(
                            arB[:, pt_col:pt_col + 1024], ps[:, 0:1024], AF.Exp,
                            bias=consts[:, C_EBI:C_EBI + 1], scale=consts[:, C_ESC:C_ESC + 1]))(ps, pt_col),
                        waits=[("pe", sc[kt]), ("pe", pt_free[kt % 2])], inc="act")
                    if kt % 2 == 0:
                        bankA_free_act = expd[kt]
                    else:
                        bankB_free_act = expd[kt]

                def emit_av(kt):
                    nonlocal av_last
                    pt_col = PT0 if kt % 2 == 0 else PT1
                    w = [("act", expd[kt])]
                    if kt == 0:
                        w += [("dve", oa_mul_free[par])]
                    p.em("tensor",
                         (lambda kt, pt_col, oA: lambda e: e.matmul(
                             oA[0:65, 0:512], arB[:, VTA0 + kt * 65: VTA0 + (kt + 1) * 65],
                             arB[:, pt_col:pt_col + 512],
                             start=(kt == 0), stop=(kt == NKT - 1), skip_group_check=True))(kt, pt_col, oA),
                         waits=w)
                    avd[kt] = av_last = p.em(
                        "tensor",
                        (lambda kt, pt_col, oB: lambda e: e.matmul(
                            oB[0:65, 0:512], arB[:, VTB0 + kt * 65: VTB0 + (kt + 1) * 65],
                            arB[:, pt_col + 512:pt_col + 1024],
                            start=(kt == 0), stop=(kt == NKT - 1), skip_group_check=True))(kt, pt_col, oB),
                        inc="pe")
                    pt_free[kt % 2] = av_last

                for kt in range(NKT):
                    emit_scores(kt)
                    emit_exp(kt)
                    if kt > 0:
                        emit_av(kt - 1)
                emit_av(NKT - 1)

                if pending is not None:
                    emit_epilogue(pending)
                pending = (hp, qc, par, oA, oB, avd[NKT - 1])
                gqc += 1
            yp_free = sc_last
            vt_free = av_last

        emit_epilogue(pending)
        attn_mul_done = p.em("vector", dve_fence(), inc="dve")
        xnT_lastread_pe = sc_last

        if mode == "pair0":
            W = [("dve", attn_mul_done)]
            for fc in range(NFC):
                p.em("sync",
                     (lambda fc: lambda e: e.dma_start(out=d_dbga[:, fc * 2048:(fc + 1) * 2048],
                                                       in_=xnT[:, fc * 2048:(fc + 1) * 2048]))(fc),
                     waits=W, inc="st", amt=16)
            for src_ap, c0, w_ in ((arB[:, YP0:YP0 + 2048], 0, 2048),
                                   (arB[:, VTA0:VTA0 + 1040], 2048, 1040),
                                   (arB[:, VTB0:VTB0 + 1040], 3088, 1040),
                                   (arB[:, PT0:PT0 + 1024], 4128, 1024),
                                   (arB[:, PT1:PT1 + 1024], 5152, 1024),
                                   (aT[:, 0:2048], 6176, 2048),
                                   (Gp[:, :], 8224, 128),
                                   (WvTp[:, :], 8352, 128),
                                   (ident[:, :], 8480, 128)):
                p.em("sync",
                     (lambda src_ap, c0, w_: lambda e: e.dma_start(out=d_dbgb[:, c0:c0 + w_], in_=src_ap))(src_ap, c0, w_),
                     waits=W, inc="st", amt=16)
            p.em("sync", lambda e: e.dma_start(out=d_dbgr[:, 0:32], in_=st1[:, 0:32]),
                 waits=W, inc="st", amt=16)
            p.em("sync", lambda e: e.dma_start(out=d_dbgs[:, 0:8], in_=consts[:, :]),
                 waits=W, inc="st", amt=16)
            p.em("sync", lambda e: e.dma_start(out=d_dbgs[:, 8:40], in_=stx[:, 160:192]),
                 waits=W, inc="st", amt=16)
            p.em("sync", lambda e: e.wait_ge(sems["st"], p.c["st"]))
            emit_block(nc, p, sems)
            return nc

        op_x = [0] * NT
        op_bank_free = [attn_mul_done, attn_mul_done]
        opA_free_act = [bankA_free_act, bankB_free_act]
        dp_last = 0
        for qt in range(NT):
            ps = (psA, psB)[qt % 2]
            for fo in range(2):
                w = [("gs", wout_ld), ("gs", bo_ld), ("act", opA_free_act[qt % 2]),
                     ("dve", op_bank_free[qt % 2]), ("dve", attn_mul_done)]
                p.em("tensor",
                     (lambda ps, fo: lambda e: e.matmul(ps[:, fo * 512:(fo + 1) * 512],
                                                        ones_bf[0:1, 0:128], bo_bf[0:1, fo * 512:(fo + 1) * 512],
                                                        start=True, stop=False))(ps, fo),
                     waits=w)
                for fc in range(NFC):
                    dp_last = p.em(
                        "tensor",
                        (lambda ps, fo, fc, qt: lambda e: e.matmul(
                            ps[:, fo * 512:(fo + 1) * 512],
                            aT[:, fc * 2048 + qt * 128: fc * 2048 + (qt + 1) * 128],
                            arB[:, WOUT0 + fc * 1024 + fo * 512: WOUT0 + fc * 1024 + fo * 512 + 512],
                            start=False, stop=(fc == NFC - 1)))(ps, fo, fc, qt),
                        inc="pe")
            opA_free_act[qt % 2] = 0
            op_bank_free[qt % 2] = op_x[qt] = p.em(
                "vector",
                (lambda ps, qt: lambda e: e.scalar_tensor_tensor(
                    x_sb[:, qt * 1024:(qt + 1) * 1024], ps[:, 0:1024], consts[:, C_ONE:C_ONE + 1],
                    x_sb[:, qt * 1024:(qt + 1) * 1024], ALU.mult, ALU.add))(ps, qt),
                waits=[("pe", dp_last)], inc="dve")
        outproj_pe_done = dp_last

        if not do_ffn:
            for fc in range(NFC):
                p.em("sync",
                     (lambda fc: lambda e: e.dma_start(out=d_dbga[:, fc * 2048:(fc + 1) * 2048],
                                                       in_=aT[:, fc * 2048:(fc + 1) * 2048]))(fc),
                     waits=[("dve", attn_mul_done)], inc="st", amt=16)
            for t in range(NT):
                p.em("sync",
                     (lambda t: lambda e: e.dma_start(out=d_out[t * 128:(t + 1) * 128, :],
                                                      in_=x_sb[:, t * 1024:(t + 1) * 1024]))(t),
                     waits=[("dve", op_x[t])], inc="st", amt=16)
            p.em("sync", lambda e: e.wait_ge(sems["st"], p.c["st"]))
            emit_block(nc, p, sems)
        else:
            op_f = p.em("vector", dve_fence(), inc="dve")
            ln2_waits = [op_x[t + 1] for t in range(NT - 1)] + [op_f]
            ln2_evac, _, ln2_tp = emit_ln(st2, "dve", ln2_waits, aT, ("dve", attn_mul_done),
                                          dump_xn2=SPARSE)
            xn2_done = [ln2_evac[t][NFC - 1] for t in range(NT)]

            def bc3(ap16):
                return bass.AP(ap16.tensor, ap16.offset, list(ap16.ap) + [[0, 8]])

            def t3(ap128):
                stp, cnt = ap128.ap[0]
                return bass.AP(ap128.tensor, ap128.offset, [[stp, cnt], [8, 16], [1, 8]])

            rt_ps_free = [0, 0]
            lg_mm = [0] * NT
            for t in range(NT):
                ps = (psC, psD)[t % 2]
                w = [("gs", br_ld), ("gs", wr_ld), ("dve", xn2_done[t]),
                     ("dve", rt_ps_free[t % 2])]
                p.em("tensor",
                     (lambda ps: lambda e: e.matmul(ps[:, 0:8], ones_bf[0:1, 0:128], br_bf[0:1, 0:8],
                                                    start=True, stop=False))(ps),
                     waits=w)
                for fc in range(NFC):
                    lg_mm[t] = p.em(
                        "tensor",
                        (lambda ps, fc, t: lambda e: e.matmul(
                            ps[:, 0:8], aT[:, fc * 2048 + t * 128: fc * 2048 + (t + 1) * 128],
                            Wr_bf[:, fc * 8:(fc + 1) * 8], start=False, stop=(fc == NFC - 1)))(ps, fc, t),
                        inc="pe")
                c0 = t * 8
                rt_ps_free[t % 2] = p.em(
                    "vector",
                    (lambda ps, c0: lambda e: e.tensor_copy(lgA[:, c0:c0 + 8], ps[:, 0:8]))(ps, c0),
                    waits=[("pe", lg_mm[t])], inc="dve")
            lg_f = p.em("vector", dve_fence(lgM), inc="dve")
            lg_dump = 0
            if mode == "rt":
                lg_dump = p.em("sync", lambda e: e.dma_start(out=d_dbgl[:, :], in_=lgA[:, :]),
                               waits=[("dve", lg_f)], inc="st", amt=16)
            p.em("vector", lambda e: e.reduce_max(wstp[:, 0:16], t3(lgA[:, :]), AX.X))
            p.em("vector", lambda e: e.tensor_tensor(t3(is1t[:, :]), t3(lgA[:, :]),
                                                     bc3(wstp[:, 0:16]), ALU.is_ge))
            p.em("vector", lambda e: e.scalar_tensor_tensor(lgM[:, :], is1t[:, :],
                                                            consts[:, C_BIG:C_BIG + 1], lgA[:, :],
                                                            ALU.mult, ALU.add))
            p.em("vector", lambda e: e.reduce_max(wstp[:, 160:176], t3(lgM[:, :]), AX.X))
            p.em("vector", lambda e: e.tensor_tensor(t3(is2t[:, :]), t3(lgM[:, :]),
                                                     bc3(wstp[:, 160:176]), ALU.is_ge))
            p.em("vector", lambda e: e.tensor_sub(wstp[:, 320:480], wstp[:, 0:160], wstp[:, 160:320]))
            d_done = p.em("vector", dve_fence(), waits=([("st", lg_dump)] if lg_dump else ()),
                          inc="dve")
            p.em("scalar", lambda e: e.activation(wstp[:, 480:640], wstp[:, 320:480], AF.Exp,
                                                  scale=consts[:, C_NEG:C_NEG + 1]),
                 waits=[("dve", d_done)])
            e_done = p.em("scalar", act_fence(), inc="act")
            p.em("vector", lambda e: e.tensor_scalar(wstp[:, 640:800], wstp[:, 480:640],
                                                     consts[:, C_ONE:C_ONE + 1], None, ALU.add),
                 waits=[("act", e_done)])
            p.em("vector", lambda e: e.reciprocal(wstp[:, 800:960], wstp[:, 640:800]))
            w_done = p.em("vector", lambda e: e.tensor_scalar(wstp[:, 960:1120], wstp[:, 800:960],
                                                              consts[:, C_ONE:C_ONE + 1],
                                                              consts[:, C_NEG:C_NEG + 1],
                                                              ALU.subtract, ALU.mult), inc="dve")
            if not SPARSE:
                p.em("vector", lambda e: e.tensor_tensor(t3(lgM[:, :]), t3(is1t[:, :]),
                                                         bc3(wstp[:, 800:816]), ALU.mult))
                p.em("vector", lambda e: e.tensor_tensor(t3(gate_sb[:, :]), t3(is2t[:, :]),
                                                         bc3(wstp[:, 960:976]), ALU.mult))
                gate_final = p.em("vector", lambda e: e.tensor_tensor(gate_sb[:, :], gate_sb[:, :],
                                                                      lgM[:, :], ALU.add), inc="dve")
                gb_x = [gate_final] * NT
            else:
                rFD = FD_B.bitcast(F32)
                r_ping = stx[:, 0:192]
                r_pong = stx[:, 192:384]
                p.em("vector", lambda e: e.tensor_sub(stx[:, 0:64], ebase[:, 0:64], ebase[:, 0:64]),
                     waits=[("dve", w_done), ("ld", const_ld)])
                p.em("vector", lambda e: e.tensor_sub(stx[:, 192:256], ebase[:, 0:64], ebase[:, 0:64]))
                p.em("vector", lambda e: e.tensor_add(stx[:, 64:192], is1t[:, :], is2t[:, :]))
                p.em("vector", lambda e: e.tensor_add(r_pong[:, 64:192], r_ping[:, 64:192],
                                                      r_ping[:, 56:184]))
                p.em("vector", lambda e: e.tensor_add(r_ping[:, 64:192], r_pong[:, 64:192],
                                                      r_pong[:, 48:176]))
                p.em("vector", lambda e: e.tensor_add(r_pong[:, 64:192], r_ping[:, 64:192],
                                                      r_ping[:, 32:160]))
                p.em("vector", lambda e: e.tensor_add(r_ping[:, 64:192], r_pong[:, 64:192],
                                                      r_pong[:, 0:128]))
                p.em("vector", lambda e: e.tensor_sub(stx[:, 384:512], r_ping[:, 64:192], is1t[:, :]))
                cum_d = p.em("vector", lambda e: e.tensor_sub(stx[:, 512:640], stx[:, 384:512],
                                                              is2t[:, :]), inc="dve")
                crossm = p.em("tensor", lambda e: e.matmul(psE[:, 0:8], stri[:, :],
                                                           r_ping[:, 184:192], start=True, stop=True),
                              waits=[("dve", cum_d), ("ld", const_ld),
                                     ("act", ln2_tp[0]), ("dve", ln2_tp[1])], inc="pe")
                def bc8(ap8):
                    return bass.AP(ap8.tensor, ap8.offset, [list(ap8.ap[0]), [0, 16], [1, 8]])
                p.em("vector", lambda e: e.tensor_tensor(t3(stx[:, 640:768]), t3(stx[:, 512:640]),
                                                         bc8(psE[:, 0:8]), ALU.add),
                     waits=[("pe", crossm)])
                p.em("vector", lambda e: e.tensor_add(stx[:, 768:896], stx[:, 640:768], ebase[:, :]))
                p.em("vector", lambda e: e.tensor_scalar(stx[:, 896:1024], stx[:, 640:768],
                                                         consts[:, C_CAP:C_CAP + 1], None, ALU.is_ge))
                p.em("vector", lambda e: e.scalar_tensor_tensor(rFD[:, 0:128], stx[:, 896:1024],
                                                                consts[:, C_HG:C_HG + 1], stx[:, 768:896],
                                                                ALU.mult, ALU.add))
                p.em("vector", lambda e: e.tensor_scalar(rFD[:, 0:128], rFD[:, 0:128],
                                                         consts[:, C_TR:C_TR + 1], None, ALU.min))
                p.em("vector", lambda e: e.tensor_mul(rFD[:, 128:256], is1t[:, :], rFD[:, 0:128]))
                p.em("vector", lambda e: e.tensor_mul(rFD[:, 256:384], is2t[:, :], rFD[:, 0:128]))
                p.em("vector", lambda e: e.tensor_sub(rFD[:, 384:512], ebase[:, :], ebase[:, :]))
                p.em("vector", lambda e: e.reduce_sum(rFD[:, 384:400], t3(rFD[:, 128:256]), AX.X))
                p.em("vector", lambda e: e.reduce_sum(rFD[:, 400:416], t3(rFD[:, 256:384]), AX.X))
                flati_d = p.em("vector", lambda e: e.tensor_tensor(flati[:, :], rFD[:, 384:512],
                                                                   rFD[:, 384:512], ALU.max), inc="dve")
                flati_f = p.em("vector", dve_fence(), inc="dve")
                if mode == "rt":
                    p.em("sync", lambda e: e.dma_start(out=d_dbgf[:, :], in_=flati[:, :]),
                         waits=[("dve", flati_f)], inc="st", amt=16)
                    p.em("sync", lambda e: e.dma_start(out=d_dbgw[:, :], in_=wstp[:, :]),
                         inc="st", amt=16)
                    p.em("sync", lambda e: e.dma_start(out=d_dbgi[:, 0:128], in_=is1t[:, :]),
                         inc="st", amt=16)
                    p.em("sync", lambda e: e.dma_start(out=d_dbgi[:, 128:256], in_=is2t[:, :]),
                         inc="st", amt=16)
                    p.em("sync", lambda e: e.dma_start(out=d_dbgx[:, :], in_=stx[:, :]),
                         inc="st", amt=16)
                scatA = scatB = 0
                scat_prev = [0, 0]
                for t in range(NT):
                    par = t % 2
                    gsem = "gliA" if par == 0 else "gliB"
                    w = [("glx", 16 * NT), ("fl", fl_total), ("dve", flati_f)]
                    if t >= 2:
                        w.append((gsem, scat_prev[par]))
                    ld_c = p.em(
                        "gpsimd",
                        (lambda t, par: lambda e: e.dma_start(
                            out=gxb[:, par * 1024:(par + 1) * 1024],
                            in_=d_xn2[t * 128:(t + 1) * 128, :]))(t, par),
                        waits=w, inc=gsem, amt=16)
                    p.em("gpsimd",
                         (lambda t, par: lambda e: e.indirect_dma_start(
                             out=d_gxall[:, :],
                             out_offset=bass.IndirectOffsetOnAxis(flati[:, t:t + 1], 0),
                             in_=gxb[:, par * 1024:(par + 1) * 1024], in_offset=None))(t, par),
                         waits=[(gsem, ld_c)], inc=gsem, amt=16)
                    s2 = p.em(
                        "gpsimd",
                        (lambda t, par: lambda e: e.indirect_dma_start(
                            out=d_gxall[:, :],
                            out_offset=bass.IndirectOffsetOnAxis(flati[:, 16 + t:17 + t], 0),
                            in_=gxb[:, par * 1024:(par + 1) * 1024], in_offset=None))(t, par),
                        inc=gsem, amt=16)
                    scat_prev[par] = s2
                    if par == 0:
                        scatA = s2
                    else:
                        scatB = s2

            h_free_pe = 0
            h_free_par = [0, 0]
            w1_ld = [0] * NE
            w2_ld = [0] * NE
            h_last_pe = [0] * NE
            y_last_pe = [0] * NE
            relu_last = [0] * (NE + 2)
            ybank_free = [rt_ps_free[0], rt_ps_free[1]]
            x_final = [[0, 0] for _ in range(NT)]
            hb_aps = [(psA, 0), (psA, 512), (psB, 0), (psB, 512)]
            hbank_free = [0, 0, 0, 0]
            hbank_free_a = [opA_free_act[0], opA_free_act[1], opA_free_act[0], opA_free_act[1]]
            hbank_free_d = [op_bank_free[0], op_bank_free[1], op_bank_free[0], op_bank_free[1]]
            gx_free = [0, 0]
            af_free = [0, 0]
            tp_bank_free = [ln2_tp[0], ln2_tp[1]]
            af32 = aT.bitcast(F32)
            lg_last = lg_mm[NT - 1] if SPARSE else 0
            ybank_free2 = [0, 0]
            y_rt_free = [rt_ps_free[0], rt_ps_free[1]]
            gy_store = [0, 0]

            for ex in range(NE):
                if ex == 0:
                    w1_wait = [("pe", xnT_lastread_pe)]
                    w2_wait_hi = [("pe", av_last), ("pe", sc_last)]
                    w2_wait_lo = [("pe", outproj_pe_done)]
                else:
                    w1_wait = [("pe", h_last_pe[ex - 1])]
                    w2_wait_hi = [("pe", y_last_pe[ex - 1])]
                    w2_wait_lo = [("pe", y_last_pe[ex - 1])]
                if SPARSE:
                    for fc in range(NFC):
                        w1_ld[ex] = p.em(
                            "sync",
                            (lambda ex, fc: lambda e: e.dma_start(
                                out=xnT[:, fc * 2048:(fc + 1) * 2048],
                                in_=d_w1b[ex, fc * 128:(fc + 1) * 128, :]))(ex, fc),
                            waits=(list(w1_wait) + [("pc", pc_total)]) if fc == 0 else (),
                            inc="gl1", amt=16)
                    b2x = (b2e, b2eB)[ex % 2]
                    b2w = () if ex < 2 else [("pe", y_last_pe[ex - 2])]
                    p.em("gpsimd",
                         (lambda ex, b2x: lambda e: e.dma_start(out=b2x[:, :],
                                                                in_=d_b2[0:1, ex * 1024:(ex + 1) * 1024]))(ex, b2x),
                         waits=b2w, inc="gl2", amt=16)
                    for fhc in range(NFH):
                        if fhc == 0:
                            w2w = list(w2_wait_lo) + [("pc", pc_total)]
                        elif fhc == 8:
                            w2w = w2_wait_hi
                        else:
                            w2w = ()
                        w2_ld[ex] = p.em(
                            "sync",
                            (lambda ex, fhc: lambda e: e.dma_start(
                                out=arB[:, fhc * 1024:(fhc + 1) * 1024],
                                in_=d_w2b[ex, fhc * 128:(fhc + 1) * 128, :]))(ex, fhc),
                            waits=w2w, inc="gl2", amt=16)
                else:
                    for fc in range(NFC):
                        w1_ld[ex] = p.em(
                            "gpsimd",
                            (lambda ex, fc: lambda e: e.dma_start(
                                out=xnT[:, fc * 2048:(fc + 1) * 2048],
                                in_=d_w1[ex, fc * 128:(fc + 1) * 128, :]))(ex, fc),
                            waits=w1_wait if fc == 0 else (), inc="gl1", amt=16)
                    p.em("gpsimd",
                         (lambda ex: lambda e: e.dma_start(out=b2e[:, :],
                                                           in_=d_b2[0:1, ex * 1024:(ex + 1) * 1024]))(ex),
                         waits=w2_wait_lo, inc="gl2", amt=16)
                    for fhc in range(NFH):
                        w2_ld[ex] = p.em(
                            "gpsimd",
                            (lambda ex, fhc: lambda e: e.dma_start(
                                out=arB[:, fhc * 1024:(fhc + 1) * 1024],
                                in_=d_w2[ex, fhc * 128:(fhc + 1) * 128, :]))(ex, fhc),
                            waits=(w2_wait_hi if fhc == 8 else ()),
                            inc="gl2", amt=16)

                if not SPARSE:
                    for qc in range(NQC):
                        re = [0] * NFH
                        for fhc in range(NFH):
                            slot = fhc % 4
                            ps, c0 = hb_aps[slot]
                            w = [("act", hbank_free[slot]), ("act", hbank_free_a[slot]),
                                 ("dve", hbank_free_d[slot]),
                                 ("gl1", w1_ld[ex]), ("dve", xn2_done[qc * 4 + 3]),
                                 ("pe", 0)]
                            for fc in range(NFC):
                                hm = p.em(
                                    "tensor",
                                    (lambda ps, c0, ex, qc, fhc, fc: lambda e: e.matmul(
                                        ps[:, c0:c0 + 512],
                                        xnT[:, fc * 2048 + fhc * 128: fc * 2048 + (fhc + 1) * 128],
                                        aT[:, fc * 2048 + qc * 512: fc * 2048 + (qc + 1) * 512],
                                        start=(fc == 0), stop=(fc == NFC - 1)))(ps, c0, ex, qc, fhc, fc),
                                    waits=(w if fc == 0 else ()), inc=("pe" if fc == NFC - 1 else None))
                            h_last_pe[ex] = hm
                            wre = [("pe", hm), ("ld", const_ld), ("pe", h_free_pe)]
                            re[fhc] = p.em(
                                "scalar",
                                (lambda ps, c0, ex, fhc: lambda e: e.activation(
                                    h_sb[:, fhc * 512:(fhc + 1) * 512], ps[:, c0:c0 + 512], AF.Relu,
                                    bias=b1t_s[:, ex * 16 + fhc:ex * 16 + fhc + 1],
                                    scale=consts[:, C_ONE:C_ONE + 1]))(ps, c0, ex, fhc),
                                waits=wre, inc="act")
                            hbank_free[slot] = re[fhc]
                            hbank_free_a[slot] = 0
                            hbank_free_d[slot] = 0
                        relu_f = p.em("scalar", act_fence(ffn=True), inc="act")
                        relu_last[ex] = relu_f
                        for tsub in range(4):
                            tt = qc * 4 + tsub
                            for fo in range(2):
                                b = (tsub * 2 + fo) % 2
                                ps = (psC, psD)[b]
                                w = [("act", relu_f), ("gl2", w2_ld[ex]),
                                     ("dve", ybank_free[b])]
                                p.em("tensor",
                                     (lambda ps, fo: lambda e: e.matmul(
                                         ps[:, 0:512], ones_bf[0:1, 0:128],
                                         b2e[0:1, fo * 512: fo * 512 + 512],
                                         start=True, stop=False))(ps, fo),
                                     waits=w)
                                for fhc in range(NFH):
                                    ym = p.em(
                                        "tensor",
                                        (lambda ps, fhc, tsub, fo: lambda e: e.matmul(
                                            ps[:, 0:512],
                                            h_sb[:, fhc * 512 + tsub * 128: fhc * 512 + (tsub + 1) * 128],
                                            arB[:, fhc * 1024 + fo * 512: fhc * 1024 + fo * 512 + 512],
                                            start=False, stop=(fhc == NFH - 1)))(ps, fhc, tsub, fo),
                                        inc=("pe" if fhc == NFH - 1 else None))
                                y_last_pe[ex] = ym
                                ybank_free[b] = x_final[tt][fo] = p.em(
                                    "vector",
                                    (lambda ps, tt, fo, ex: lambda e: e.scalar_tensor_tensor(
                                        x_sb[:, tt * 1024 + fo * 512: tt * 1024 + fo * 512 + 512],
                                        ps[:, 0:512], gate_sb[:, tt * 8 + ex: tt * 8 + ex + 1],
                                        x_sb[:, tt * 1024 + fo * 512: tt * 1024 + fo * 512 + 512],
                                        ALU.mult, ALU.add))(ps, tt, fo, ex),
                                    waits=[("pe", ym), ("dve", gb_x[tt])], inc="dve")
                        h_free_pe = y_last_pe[ex]
                else:
                    xfm = xn_tmp.bitcast(BF16)
                    for st in range(NST):
                        par = st % 2
                        gsem = "gliA" if par == 0 else "gliB"
                        base = ex * CAP + st * 128
                        w = [("gliA", scatA), ("gliB", scatB), ("dve", gx_free[par])]
                        gx_c = p.em(
                            "gpsimd",
                            (lambda base, par: lambda e: e.dma_start(
                                out=gxb[:, par * 1024:(par + 1) * 1024],
                                in_=d_gxall[base:base + 128, :]))(base, par),
                            waits=w, inc=gsem, amt=16)
                        upw = [(gsem, gx_c), ("pe", af_free[par])]
                        if ex == 0 and st <= 1:
                            upw.append(("pe", lg_last))
                        upc = p.em(
                            "vector",
                            (lambda par: lambda e: e.tensor_copy(
                                af32[:, par * 1024:(par + 1) * 1024],
                                gxb[:, par * 1024:(par + 1) * 1024]))(par),
                            waits=upw, inc="dve")
                        gx_free[par] = upc
                        tp_cnt = [0] * (NFC + 1)
                        ev = [0] * NFC
                        for fc in range(NFC + 1):
                            bpar = fc % 2
                            tps = psE if bpar == 0 else psF
                            w = [(("act", "dve")[bpar], tp_bank_free[bpar]), ("ld", const_ld)]
                            if fc == 0:
                                w.append(("dve", upc))
                                if ex == 0 and st <= 1:
                                    w.append(("dve", flati_f))
                            src_col = (fc % NFC) * 128
                            tp_cnt[fc] = p.em(
                                "tensor",
                                (lambda tps, par, src_col: lambda e: e.transpose(
                                    tps[:, 0:128],
                                    af32[:, par * 1024 + src_col: par * 1024 + src_col + 128],
                                    identf[:, :]))(tps, par, src_col),
                                waits=w, inc="pe")
                            if fc >= 1:
                                pfc = fc - 1
                                pps = psE if pfc % 2 == 0 else psF
                                eng, snm = ("scalar", "act") if pfc % 2 == 0 else ("vector", "dve")
                                tp_bank_free[pfc % 2] = ev[pfc] = p.em(
                                    eng,
                                    (lambda pps, par, pfc: lambda e: (
                                        e.tensor_copy if hasattr(e, 'tensor_copy') else e.copy)(
                                        xfm[:, par * 1024 + pfc * 128: par * 1024 + (pfc + 1) * 128],
                                        pps[:, 0:128]))(pps, par, pfc),
                                    waits=[("pe", tp_cnt[pfc + 1])], inc=snm)
                        af_free[par] = tp_cnt[NFC]
                        ev_dve = ev[NFC - 1]
                        ev_act = ev[NFC - 2]
                        hst = [0] * NFH
                        for fhc in range(NFH):
                            slot = fhc % 4
                            ps, c0 = hb_aps[slot]
                            w = [("dve", ev_dve if fhc == 0 else 0),
                                 ("act", ev_act if fhc == 0 else 0),
                                 ("dve", hbank_free[slot]),
                                 ("act", hbank_free_a[slot]), ("dve", hbank_free_d[slot]),
                                 ("gl1", w1_ld[ex])]
                            for fc in range(NFC):
                                hm = p.em(
                                    "tensor",
                                    (lambda ps, c0, fhc, fc, par: lambda e: e.matmul(
                                        ps[:, c0:c0 + 128],
                                        xnT[:, fc * 2048 + fhc * 128: fc * 2048 + (fhc + 1) * 128],
                                        xfm[:, par * 1024 + fc * 128: par * 1024 + (fc + 1) * 128],
                                        start=(fc == 0), stop=(fc == NFC - 1)))(ps, c0, fhc, fc, par),
                                    waits=(w if fc == 0 else ()), inc=("pe" if fc == NFC - 1 else None))
                            h_last_pe[ex] = hm
                            hst[fhc] = hbank_free[slot] = p.em(
                                "vector",
                                (lambda ps, c0, ex, fhc, par: lambda e: e.scalar_tensor_tensor(
                                    h_sb[:, par * 2048 + fhc * 128: par * 2048 + (fhc + 1) * 128],
                                    ps[:, c0:c0 + 128],
                                    b1t_s[:, ex * 16 + fhc:ex * 16 + fhc + 1],
                                    z128f[:, :], ALU.add, ALU.max))(ps, c0, ex, fhc, par),
                                waits=[("pe", hm), ("pe", h_free_par[par]), ("ld", const_ld)], inc="dve")
                            hbank_free_a[slot] = 0
                            hbank_free_d[slot] = 0
                        relu_f = p.em("vector", dve_fence(), inc="dve")
                        for fo in range(2):
                            b = fo
                            ps = (psC, psD)[b]
                            w = [("dve", relu_f), ("gl2", w2_ld[ex]), ("act", ybank_free2[b]),
                                 ("dve", y_rt_free[b])]
                            p.em("tensor",
                                 (lambda ps, fo, b2x: lambda e: e.matmul(
                                     ps[:, 0:512], ones_bf[0:1, 0:128],
                                     b2x[0:1, fo * 512: fo * 512 + 512],
                                     start=True, stop=False))(ps, fo, b2x),
                                 waits=w)
                            for fhc in range(NFH):
                                ym = p.em(
                                    "tensor",
                                    (lambda ps, fhc, par, fo: lambda e: e.matmul(
                                        ps[:, 0:512],
                                        h_sb[:, par * 2048 + fhc * 128: par * 2048 + (fhc + 1) * 128],
                                        arB[:, fhc * 1024 + fo * 512: fhc * 1024 + fo * 512 + 512],
                                        start=False, stop=(fhc == NFH - 1)))(ps, fhc, par, fo),
                                    inc=("pe" if fhc == NFH - 1 else None))
                            y_last_pe[ex] = ym
                            ybank_free2[b] = p.em(
                                "scalar",
                                (lambda ps, par, fo: lambda e: e.copy(
                                    gyb[:, par * 1024 + fo * 512: par * 1024 + fo * 512 + 512],
                                    ps[:, 0:512]))(ps, par, fo),
                                waits=[("pe", ym), ("glxA" if par == 0 else "glxB", gy_store[par])],
                                inc="act")
                            y_rt_free[b] = 0
                        gy_store[par] = p.em(
                            "scalar",
                            (lambda base, par: lambda e: e.dma_start(
                                out=d_gy[base:base + 128, :],
                                in_=gyb[:, par * 1024:(par + 1) * 1024]))(base, par),
                            inc="glxA" if par == 0 else "glxB", amt=16)
                        h_free_par[par] = y_last_pe[ex]

            if SPARSE:
                gA_tot = p.c["glxA"]
                gB_tot = p.c["glxB"]
                g_free = [0, 0]
                for t in range(NT):
                    par = t % 2
                    gsem = "gliA" if par == 0 else "gliB"
                    w = [("glxA", gA_tot), ("glxB", gB_tot), ("dve", g_free[par]),
                         ("dve", flati_f), ("dve", gx_free[par])]
                    p.em("gpsimd",
                         (lambda t, par: lambda e: e.indirect_dma_start(
                             out=gxb[:, par * 1024:(par + 1) * 1024], out_offset=None,
                             in_=d_gy[:, :],
                             in_offset=bass.IndirectOffsetOnAxis(flati[:, t:t + 1], 0)))(t, par),
                         waits=w, inc=gsem, amt=16)
                    g2c = p.em(
                        "gpsimd",
                        (lambda t, par: lambda e: e.indirect_dma_start(
                            out=gyb[:, par * 1024:(par + 1) * 1024], out_offset=None,
                            in_=d_gy[:, :],
                            in_offset=bass.IndirectOffsetOnAxis(flati[:, 16 + t:17 + t], 0)))(t, par),
                        inc=gsem, amt=16)
                    p.em("vector",
                         (lambda t, par: lambda e: e.scalar_tensor_tensor(
                             x_sb[:, t * 1024:(t + 1) * 1024],
                             gxb[:, par * 1024:(par + 1) * 1024],
                             wstp[:, 800 + t:801 + t],
                             x_sb[:, t * 1024:(t + 1) * 1024], ALU.mult, ALU.add))(t, par),
                         waits=[(gsem, g2c)], inc="dve")
                    s2 = p.em(
                        "vector",
                        (lambda t, par: lambda e: e.scalar_tensor_tensor(
                            x_sb[:, t * 1024:(t + 1) * 1024],
                            gyb[:, par * 1024:(par + 1) * 1024],
                            wstp[:, 960 + t:961 + t],
                            x_sb[:, t * 1024:(t + 1) * 1024], ALU.mult, ALU.add))(t, par),
                        inc="dve")
                    g_free[par] = s2
                    x_final[t] = [s2, s2]

            st_f = p.em("vector", dve_fence(), inc="dve")
            for t in range(NT):
                p.em("sync",
                     (lambda t: lambda e: e.dma_start(out=d_out[t * 128:(t + 1) * 128, :],
                                                      in_=x_sb[:, t * 1024:(t + 1) * 1024]))(t),
                     waits=[("dve", x_final[t][0]), ("dve", x_final[t][1])], inc="st", amt=16)
            p.em("sync", lambda e: e.wait_ge(sems["st"], p.c["st"]))
            emit_block(nc, p, sems)

    return nc


def emit_block(nc, p, sems):
    with nc.Block() as block:
        def mk(name):
            def run(eng):
                seen = {}
                for fn, waits, inc, amt in p.streams[name]:
                    for s, v in waits:
                        if v > seen.get(s, 0):
                            eng.wait_ge(sems[s], v)
                            seen[s] = v
                    ins = fn(eng)
                    if inc is not None:
                        ins.then_inc(sems[inc], amt)
            return run
        block.sync(mk("sync"))
        block.gpsimd(mk("gpsimd"))
        block.tensor(mk("tensor"))
        block.scalar(mk("scalar"))
        block.vector(mk("vector"))


_CACHE = {}


def _get_nc(do_ffn=True, mode="full"):
    key = (do_ffn, mode)
    if key not in _CACHE:
        _CACHE[key] = build_nc(do_ffn=do_ffn, mode=mode)
    return _CACHE[key]


def kernel(L0_attn_x, L0_attn_Wq, L0_attn_Wk, L0_attn_Wv, L0_attn_W_out, L0_attn_b_out,
           L0_ffn_Wr, L0_ffn_br, L0_ffn_W1, L0_ffn_b1, L0_ffn_W2, L0_ffn_b2,
           _do_ffn=True, _mode="full"):
    x = np.ascontiguousarray(np.asarray(L0_attn_x, dtype=np.float32))
    wq = np.ascontiguousarray(np.asarray(L0_attn_Wq, dtype=np.float32))
    wk = np.ascontiguousarray(np.asarray(L0_attn_Wk, dtype=np.float32))
    wvt = np.ascontiguousarray(np.asarray(L0_attn_Wv, dtype=np.float32).T)
    woutt = np.ascontiguousarray(np.asarray(L0_attn_W_out, dtype=np.float32).T)
    bout = np.ascontiguousarray(np.asarray(L0_attn_b_out, dtype=np.float32).reshape(1, E))
    wr = np.ascontiguousarray(np.asarray(L0_ffn_Wr, dtype=np.float32))
    br = np.ascontiguousarray(np.asarray(L0_ffn_br, dtype=np.float32).reshape(1, NE))
    w1 = np.ascontiguousarray(np.asarray(L0_ffn_W1, dtype=np.float32))
    b1t = np.ascontiguousarray(np.asarray(L0_ffn_b1, dtype=np.float32).reshape(NE, NFH, 128).transpose(0, 2, 1))
    w2 = np.ascontiguousarray(np.asarray(L0_ffn_W2, dtype=np.float32))
    b2 = np.ascontiguousarray(np.asarray(L0_ffn_b2, dtype=np.float32).reshape(1, NE * E))

    import ml_dtypes
    cst = np.zeros((128, 16), np.float32)
    for col, val in ((0, 1.0 / 32.0), (1, -EXPC), (2, 1e-5), (3, 1.0 / 1024.0),
                     (4, -0.5), (5, 1.0), (6, -1.0), (7, -1e30), (8, float(CAP)),
                     (9, 8192.0), (10, float(NE * CAP))):
        cst[:, col] = val
    identb = np.eye(128, dtype=ml_dtypes.bfloat16)
    identf = np.eye(128, dtype=np.float32)
    onesb = np.ones((1, 128), ml_dtypes.bfloat16)
    onesf = np.ones((65, 64), np.float32)
    zb = np.zeros((64, 128), ml_dtypes.bfloat16)
    zbig = np.zeros((128, E), ml_dtypes.bfloat16)
    onescol = np.ones((128, 1), ml_dtypes.bfloat16)
    zf = np.zeros((128, 1120), np.float32)
    stri = np.triu(np.ones((128, 128), np.float32), 1)
    ebase = np.zeros((128, 128), np.float32)
    for t_ in range(16):
        for e_ in range(8):
            ebase[:, t_ * 8 + e_] = e_ * CAP


    nc = _get_nc(do_ffn=_do_ffn, mode=_mode)
    in_maps = []
    for c in range(8):
        in_maps.append({
            "x": np.ascontiguousarray(x[c]),
            "wq": wq, "wk": wk, "wvt": wvt, "woutt": woutt, "bout": bout,
            "wr": wr, "br": br, "w1": w1, "b1t": b1t, "w2": w2, "b2": b2,
            "cst": cst, "identb": identb, "identf": identf, "onesb": onesb, "onesf": onesf,
            "zb": zb, "onescol": onescol, "zf": zf, "stri": stri, "ebase": ebase,
            "zbig": zbig,
        })
    res = run_bass_kernel_spmd(nc, in_maps, core_ids=list(range(8)))
    if _mode == "rt":
        keys = ("out", "dbgf", "dbgl", "dbgw", "dbgi", "dbgx", "xn2_d", "gy_d", "gxall_d")
        return {k: np.stack([np.asarray(res.results[c][k]) for c in range(8)], axis=0)
                for k in keys}
    return np.stack([np.asarray(res.results[c]["out"]) for c in range(8)], axis=0)
